# version 01
# baseline (speedup 1.0000x reference)
import math
import numpy as np
import ml_dtypes
import concourse.bass as bass
import concourse.mybir as mybir
from concourse.bass_utils import run_bass_kernel_spmd
from contextlib import ExitStack

F32 = mybir.dt.float32
BF16 = mybir.dt.bfloat16
AF = mybir.ActivationFunctionType
ALU = mybir.AluOpType

f32 = np.float32

S = 4096
D = 2048
NT = S // 128
NR = S // 512
H = 16
G = 4
FF = 8192
ATT_SCALE = f32(1.0 / math.sqrt(128.0))
CSHIFT = f32(-8.0)
LN_EPS = f32(1e-5)
RMS_EPS = f32(np.finfo(np.float32).eps)
QTHR = 0.501953125
ND = 35
SLOPES = [float((2.0 ** (-0.5)) ** (i + 1)) for i in range(16)]

ALL_PHASES = ("p0", "a", "b1", "b2", "c")


class Plan:
    ENGINES = ("sync", "gpsimd", "vector", "scalar", "tensor")

    def __init__(self):
        self.items = {e: [] for e in self.ENGINES}

    def emit(self, eng, fn, waits=(), incs=()):
        self.items[eng].append((tuple(waits), fn, tuple(incs)))

    def check(self, cts):
        final = {c.sem: c.n for c in cts}
        mult = {c.sem: (16 if c.dma else 1) for c in cts}
        for e, items in self.items.items():
            for i, (waits, fn, incs) in enumerate(items):
                for sem, v in waits:
                    assert v <= final.get(sem, 0), \
                        (e, i, sem.name, v, final.get(sem, 0))
                    assert v % mult.get(sem, 1) == 0, (e, i, sem.name, v)

    def run_block(self, nc, name):
        import os as _os
        drop = set((_os.environ.get("C_DROP", "") or "").split(","))
        if name in ("c", "fin") and drop & set(self.ENGINES):
            for e in drop & set(self.ENGINES):
                self.items[e] = []
        def runner(items):
            def body(eng):
                for waits, fn, incs in items:
                    for sem, v in waits:
                        if v > 0:
                            eng.wait_ge(sem, v)
                    inst = fn(eng)
                    for sem, k in incs:
                        inst.then_inc(sem, k)
            return body
        with nc.Block(name) as blk:
            blk.sync(runner(self.items["sync"]))
            blk.gpsimd(runner(self.items["gpsimd"]))
            blk.vector(runner(self.items["vector"]))
            blk.scalar(runner(self.items["scalar"]))
            blk.tensor(runner(self.items["tensor"]))


class DST:

    def __init__(self, ct):
        self.ct = ct

    def emit(self, P, eng, fn, waits=()):
        w = list(waits)
        if self.ct.n > 0:
            w.append(self.ct.ev())
        P.emit(eng, fn, w, (self.ct.tick(16),))

    def ev(self, n=None):
        return self.ct.ev(n)

    @property
    def n(self):
        return self.ct.n


class CT:
    def __init__(self, sem):
        self.sem = sem
        self.n = 0
        self.dma = None

    def tick(self, k=1):
        if self.dma is None:
            self.dma = (k == 16)
        self.n += k
        return (self.sem, k)

    def ev(self, n=None):
        return (self.sem, self.n if n is None else n)


def build_program(phases=ALL_PHASES, dump=False):
    nc = bass.Bass("TRN2", target_bir_lowering=False, debug=False)
    skind = "ExternalOutput" if dump else "Internal"

    x_d = nc.dram_tensor("x", [S, D], F32, kind="ExternalInput").ap()
    wqT_d = nc.dram_tensor("wqT", [D, D], F32, kind="ExternalInput").ap()
    wkT_d = nc.dram_tensor("wkT", [D, 512], F32, kind="ExternalInput").ap()
    wvT_d = nc.dram_tensor("wvT", [D, 512], F32, kind="ExternalInput").ap()
    woT_d = nc.dram_tensor("woT", [D, D], F32, kind="ExternalInput").ap()
    wguT_d = nc.dram_tensor("wguT", [D, 2 * FF], F32, kind="ExternalInput").ap()
    wpT_d = nc.dram_tensor("wpT", [FF, D], F32, kind="ExternalInput").ap()
    sqd_d = nc.dram_tensor("sqd", [128, ND * 128], F32, kind="ExternalInput").ap()
    idb_d = nc.dram_tensor("idb", [128, 128], BF16, kind="ExternalInput").ap()
    idf_d = nc.dram_tensor("idf", [128, 128], F32, kind="ExternalInput").ap()
    bones_d = nc.dram_tensor("bonesc", [128, 2], BF16, kind="ExternalInput").ap()
    sel_d = nc.dram_tensor("selc", [2, 128], BF16, kind="ExternalInput").ap()
    y_d = nc.dram_tensor("y", [S, D], F32, kind="ExternalOutput").ap()

    q_sp = nc.dram_tensor("q_sp", [H, 128, S], BF16, kind=skind).ap()
    kT_sp = nc.dram_tensor("kT_sp", [G, 128, S], BF16, kind=skind).ap()
    v_sp = nc.dram_tensor("v_sp", [NT, 128, 512], BF16, kind=skind).ap()
    oT_sp = nc.dram_tensor("oT_sp", [H, 128, S], BF16, kind=skind).ap()
    l2t_sp = nc.dram_tensor("l2t_sp", [16, 128, S], BF16, kind=skind).ap()
    xm_sp = nc.dram_tensor("xm_sp", [S, D], F32, kind=skind).ap()
    wob_sp = nc.dram_tensor("wob_sp", [16, 128, D], BF16, kind=skind).ap()
    wgu_sp = nc.dram_tensor("wgu_sp", [32, 16, 128, 512], BF16, kind=skind).ap()
    wp_sp = nc.dram_tensor("wp_sp", [64, 128, D], BF16, kind=skind).ap()

    with ExitStack() as PS:
        id_b = PS.enter_context(nc.sbuf_tensor("id_b", [128, 128], BF16))
        id_f = PS.enter_context(nc.sbuf_tensor("id_f", [128, 128], F32))
        ones_col = PS.enter_context(nc.sbuf_tensor("ones_col", [128, 1], BF16))
        ones_row = PS.enter_context(nc.sbuf_tensor("ones_row", [1, 128], BF16))
        ones_full = PS.enter_context(nc.sbuf_tensor("ones_full", [128, 128], BF16))
        bones = PS.enter_context(nc.sbuf_tensor("bones", [128, 2], BF16))
        sel = PS.enter_context(nc.sbuf_tensor("sel", [2, 128], BF16))

        all_cts = []

        def mksem(name):
            c = CT(PS.enter_context(nc.semaphore(name)))
            all_cts.append(c)
            return c

        d_y = [DST(mksem(f"y_st{i}")) for i in range(2)]

        if "p0" in phases:
            P = Plan()
            with ExitStack() as SS:
                wo_tmp = [SS.enter_context(nc.sbuf_tensor(f"wotmp{i}", [128, D], BF16))
                          for i in range(2)]
                c_ld = mksem("p0a_ld")
                d_cast = [DST(mksem(f"p0a_cast{i}")) for i in range(2)]
                d_st = [DST(mksem(f"p0a_st{i}")) for i in range(2)]
                P.emit("sync", lambda e: e.dma_start(id_b[:], idb_d), (), (c_ld.tick(16),))
                P.emit("sync", lambda e: e.dma_start(id_f[:], idf_d), (), (c_ld.tick(16),))
                P.emit("vector", lambda e: e.memset(ones_col[:], 1.0))
                P.emit("vector", lambda e: e.memset(ones_row[:], 1.0))
                P.emit("vector", lambda e: e.memset(ones_full[:], 1.0))
                P.emit("sync", lambda e: e.dma_start(bones[:], bones_d), (),
                       (c_ld.tick(16),))
                P.emit("sync", lambda e: e.dma_start(sel[:], sel_d), (),
                       (c_ld.tick(16),))
                for of in range(16):
                    waits = [d_st[of % 2].ev()] if of >= 2 else []
                    d_cast[of % 2].emit(
                        P, "gpsimd",
                        lambda e, of=of: e.dma_start(
                            wo_tmp[of % 2][:], woT_d[128 * of:128 * (of + 1), :]),
                        waits)
                    d_st[of % 2].emit(
                        P, "scalar",
                        lambda e, of=of: e.dma_start(wob_sp[of], wo_tmp[of % 2][:]),
                        [d_cast[of % 2].ev()])
                P.emit("sync", lambda e: e.nop(),
                       [d_st[0].ev(), d_st[1].ev(), c_ld.ev()])
                P.check(all_cts); P.run_block(nc, "p0a")

        if "p0" in phases:
            P = Plan()
            with ExitStack() as SS:
                slab = [SS.enter_context(nc.sbuf_tensor(f"slab{i}", [128, FF], BF16))
                        for i in range(2)]
                aw_sb = SS.enter_context(nc.sbuf_tensor("aw_sb", [128, FF], BF16))
                sg_sb = SS.enter_context(nc.sbuf_tensor("sg_sb", [128, FF], BF16))
                sc_sb = [SS.enter_context(nc.sbuf_tensor(f"sc{i}", [2, 512], BF16))
                         for i in range(2)]
                thr_sb = [SS.enter_context(nc.sbuf_tensor(f"thr{i}", [128, 512], F32))
                          for i in range(2)]
                msk_sb = [SS.enter_context(nc.sbuf_tensor(f"msk{i}", [128, 512], BF16))
                          for i in range(2)]
                t1_sb = [SS.enter_context(nc.sbuf_tensor(f"t1_{i}", [128, 512], BF16))
                         for i in range(2)]
                wh_sb = [SS.enter_context(nc.sbuf_tensor(f"wh{i}", [128, 512], BF16))
                         for i in range(2)]
                ps_s = [SS.enter_context(nc.psum_tensor(f"ps_s{i}", [2, 512], F32))
                        for i in range(2)]
                ps_b = [SS.enter_context(nc.psum_tensor(f"ps_b{i}", [128, 512], F32))
                        for i in range(2)]
                d_ld = [DST(mksem(f"q_ld{i}")) for i in range(2)]
                c_abs = mksem("q_abs")
                c_sum = mksem("q_sum")
                c_scl = mksem("q_scl")
                c_bc = mksem("q_bc")
                c_thr = mksem("q_thr")
                c_msk = mksem("q_msk")
                c_t1 = mksem("q_t1")
                c_wh = mksem("q_wh")
                d_sp = [DST(mksem(f"q_sp{i}")) for i in range(2)]

                jobs = []
                for dc in range(16):
                    for hs in range(2):
                        src = wguT_d[128 * dc:128 * (dc + 1), FF * hs:FF * (hs + 1)]
                        dsts = [wgu_sp[16 * hs + w, dc] for w in range(16)]
                        jobs.append((src, FF, dsts))
                for c in range(64):
                    src = wpT_d[128 * c:128 * (c + 1), :]
                    dsts = [wp_sp[c][:, 512 * w:512 * (w + 1)] for w in range(4)]
                    jobs.append((src, D, dsts))
                cum = [0]
                for _, _, dsts in jobs:
                    cum.append(cum[-1] + len(dsts))

                nwin = 0
                for ji, (src, width, dsts) in enumerate(jobs):
                    waits = []
                    if ji >= 2:
                        waits.append(c_abs.ev(2 * (ji - 1)))
                    d_ld[ji % 2].emit(P, "gpsimd",
                                      lambda e, s_=src, w_=width, ji=ji:
                                      e.dma_start(slab[ji % 2][:, 0:w_], s_),
                                      waits)
                    waits = [d_ld[ji % 2].ev()]
                    if ji >= 1:
                        waits.append(c_msk.ev(cum[ji]))
                        waits.append(c_t1.ev(cum[ji]))
                    P.emit("scalar", lambda e, w_=width, ji=ji:
                           e.activation(aw_sb[:, 0:w_], slab[ji % 2][:, 0:w_], AF.Abs),
                           waits, (c_abs.tick(),))
                    P.emit("scalar", lambda e, w_=width, ji=ji:
                           e.activation(sg_sb[:, 0:w_], slab[ji % 2][:, 0:w_], AF.Sign),
                           (), (c_abs.tick(),))
                    for w, dst in enumerate(dsts):
                        nwin += 1
                        p = nwin % 2
                        waits = [c_abs.ev(2 * ji + 1)]
                        if nwin > 2:
                            waits.append(c_scl.ev(nwin - 2))
                        P.emit("tensor", lambda e, w=w, p=p:
                               e.matmul(ps_s[p][:], bones[:],
                                        aw_sb[:, 512 * w:512 * (w + 1)],
                                        start=True, stop=True, skip_group_check=True),
                               waits, (c_sum.tick(),))
                        waits = [c_sum.ev(nwin)]
                        if nwin > 2:
                            waits.append(c_bc.ev(nwin - 2))
                        P.emit("vector", lambda e, p=p:
                               e.tensor_scalar(sc_sb[p][:], ps_s[p][:],
                                               1.0 / 64.0, 1e-8, ALU.mult, ALU.max),
                               waits, (c_scl.tick(),))
                        waits = [c_scl.ev(nwin)]
                        if nwin > 2:
                            waits.append(c_wh.ev(nwin - 2))
                        P.emit("tensor", lambda e, p=p:
                               e.matmul(ps_b[p][:], sel[:], sc_sb[p][:],
                                        start=True, stop=True, skip_group_check=True),
                               waits, (c_bc.tick(),))
                        waits = [c_bc.ev(nwin)]
                        if nwin > 2:
                            waits.append(c_msk.ev(nwin - 2))
                        P.emit("vector", lambda e, p=p:
                               e.tensor_scalar_mul(thr_sb[p][:], ps_b[p][:], QTHR),
                               waits, (c_thr.tick(),))
                        P.emit("vector", lambda e, p=p, w=w:
                               e.tensor_tensor(msk_sb[p][:],
                                               aw_sb[:, 512 * w:512 * (w + 1)],
                                               thr_sb[p][:], ALU.is_gt),
                               [c_thr.ev(nwin)], (c_msk.tick(),))
                        P.emit("vector", lambda e, p=p, w=w:
                               e.tensor_tensor(t1_sb[p][:], msk_sb[p][:],
                                               sg_sb[:, 512 * w:512 * (w + 1)],
                                               ALU.mult),
                               [c_msk.ev(nwin)], (c_t1.tick(),))
                        waits = [c_t1.ev(nwin)]
                        if nwin > 2:
                            waits.append(d_sp[nwin % 2].ev())
                        P.emit("vector", lambda e, p=p, nw=nwin:
                               e.tensor_tensor(wh_sb[nw % 2][:], t1_sb[p][:],
                                               ps_b[p][:], ALU.mult),
                               waits, (c_wh.tick(),))
                        d_sp[nwin % 2].emit(P, "scalar",
                                            lambda e, dst=dst, nw=nwin:
                                            e.dma_start(dst, wh_sb[nw % 2][:]),
                                            [c_wh.ev(nwin)])
                P.emit("sync", lambda e: e.nop(), [d_sp[0].ev(), d_sp[1].ev()])
                P.check(all_cts); P.run_block(nc, "p0b")

        if "a" in phases:
            P = Plan()
            with ExitStack() as SS:
                wq_sb = SS.enter_context(nc.sbuf_tensor("wq_sb", [128, 16, D], BF16))
                wk_sb = SS.enter_context(nc.sbuf_tensor("wk_sb", [128, 16, 512], BF16))
                wv_sb = SS.enter_context(nc.sbuf_tensor("wv_sb", [128, 16, 512], BF16))
                xb = [SS.enter_context(nc.sbuf_tensor(f"xa{i}", [128, D], F32))
                      for i in range(2)]
                lnb = [SS.enter_context(nc.sbuf_tensor(f"lnb{i}", [128, D], BF16))
                       for i in range(2)]
                lxT = SS.enter_context(nc.sbuf_tensor("lxT", [128, 16, 512], BF16))
                ev_sb = [SS.enter_context(nc.sbuf_tensor(f"aev{i}", [128, 512], BF16))
                         for i in range(4)]
                bns = SS.enter_context(nc.sbuf_tensor("bns", [128, 24], F32))
                mv = SS.enter_context(nc.sbuf_tensor("mv", [128, 2], F32))
                sdr = SS.enter_context(nc.sbuf_tensor("sdr", [128, 2], F32))
                ps_tr = [SS.enter_context(nc.psum_tensor(f"ps_tr{i}", [128, 128], BF16))
                         for i in range(2)]
                ps_mm = [SS.enter_context(nc.psum_tensor(f"ps_mm{i}", [128, 512], F32))
                         for i in range(2)]
                c_w = mksem("a_w")
                d_x = [DST(mksem(f"a_x{i}")) for i in range(2)]
                c_v = mksem("a_v")
                c_s = mksem("a_s")
                c_t = mksem("a_t")
                c_te = mksem("a_te")
                c_m = mksem("a_m")
                c_qe = mksem("a_qe")
                d_o = [DST(mksem(f"a_o{i}")) for i in range(4)]

                for wsb, wd in [(wq_sb, wqT_d), (wk_sb, wkT_d), (wv_sb, wvT_d)]:
                    for dc in range(16):
                        P.emit("gpsimd", lambda e, wsb=wsb, wd=wd, dc=dc:
                               e.dma_start(wsb[:, dc, :], wd[128 * dc:128 * (dc + 1), :]),
                               (), (c_w.tick(16),))
                VPT = 7
                for t in range(NT):
                    r, tt = t // 4, t % 4
                    waits = [c_v.ev(VPT * (t - 1))] if t >= 2 else []
                    d_x[t % 2].emit(P, "sync", lambda e, t=t:
                                    e.dma_start(xb[t % 2][:],
                                                x_d[128 * t:128 * (t + 1), :]),
                                    waits)
                    for i in range(4):
                        waits = [d_x[t % 2].ev()] if i == 0 else []
                        P.emit("vector", lambda e, t=t, i=i:
                               e.bn_stats(bns[:, 6 * i:6 * (i + 1)],
                                          xb[t % 2][:, 512 * i:512 * (i + 1)]),
                               waits, (c_v.tick(),))
                    P.emit("vector", lambda e: e.bn_aggr(mv[:], bns[:]),
                           [c_v.ev()], (c_v.tick(),))
                    P.emit("scalar", lambda e:
                           e.activation(sdr[:, 0:1], mv[:, 1:2], AF.Sqrt, bias=LN_EPS),
                           [c_v.ev()], (c_s.tick(),))
                    P.emit("vector", lambda e: e.reciprocal(sdr[:, 1:2], sdr[:, 0:1]),
                           [c_s.ev()], (c_v.tick(),))
                    waits = [c_v.ev()]
                    if t >= 2:
                        waits.append(c_t.ev(16 * (t - 1)))
                    P.emit("vector", lambda e, t=t:
                           e.tensor_scalar(lnb[t % 2][:], xb[t % 2][:],
                                           mv[:, 0:1], sdr[:, 1:2],
                                           ALU.subtract, ALU.mult),
                           waits, (c_v.tick(),))
                    for dc in range(16):
                        idx = 16 * t + dc + 1
                        waits = [c_v.ev(VPT * (t + 1))]
                        if idx > 2:
                            waits.append(c_te.ev(idx - 2))
                        P.emit("tensor", lambda e, t=t, dc=dc:
                               e.transpose(ps_tr[dc % 2][:],
                                           lnb[t % 2][:, 128 * dc:128 * (dc + 1)],
                                           id_b[:]),
                               waits, (c_t.tick(),))
                        ewaits = [c_t.ev(idx), c_m.ev(24 * r)]
                        P.emit("scalar", lambda e, tt=tt, dc=dc:
                               e.copy(lxT[:, dc, 128 * tt:128 * (tt + 1)],
                                      ps_tr[dc % 2][:]),
                               ewaits, (c_te.tick(),))
                    if tt == 3:
                        for i in range(24):
                            nev = 24 * r + i
                            p = ps_mm[nev % 2]
                            waits = [c_w.ev(16 * 48), c_te.ev(64 * (r + 1))]
                            if nev >= 2:
                                waits.append(c_qe.ev(nev - 1))

                            def mk(e, i=i, p=p):
                                if i < 16:
                                    for dc in range(16):
                                        inst = e.matmul(
                                            p[:], wq_sb[:, dc, 128 * i:128 * (i + 1)],
                                            lxT[:, dc, :],
                                            start=(dc == 0), stop=(dc == 15),
                                            skip_group_check=True)
                                elif i < 20:
                                    kf = i - 16
                                    for dc in range(16):
                                        inst = e.matmul(
                                            p[:], wk_sb[:, dc, 128 * kf:128 * (kf + 1)],
                                            lxT[:, dc, :],
                                            start=(dc == 0), stop=(dc == 15),
                                            skip_group_check=True)
                                else:
                                    tv = i - 20
                                    for dc in range(16):
                                        inst = e.matmul(
                                            p[:], lxT[:, dc, 128 * tv:128 * (tv + 1)],
                                            wv_sb[:, dc, :],
                                            start=(dc == 0), stop=(dc == 15),
                                            skip_group_check=True)
                                return inst
                            P.emit("tensor", mk, waits, (c_m.tick(),))
                            buf = ev_sb[nev % 4]
                            waits = [c_m.ev(nev + 1)]
                            if nev >= 4:
                                waits.append(d_o[nev % 4].ev())
                            if i < 16:
                                P.emit("scalar", lambda e, buf=buf, p=p:
                                       e.activation(buf[:], p[:], AF.Copy,
                                                    bias=0.0, scale=ATT_SCALE),
                                       waits, (c_qe.tick(),))
                            else:
                                P.emit("scalar", lambda e, buf=buf, p=p:
                                       e.copy(buf[:], p[:]),
                                       waits, (c_qe.tick(),))
                            if i < 16:
                                dst = q_sp[i][:, 512 * r:512 * (r + 1)]
                            elif i < 20:
                                dst = kT_sp[i - 16][:, 512 * r:512 * (r + 1)]
                            else:
                                dst = v_sp[4 * r + (i - 20)]
                            d_o[nev % 4].emit(P, "scalar",
                                              lambda e, dst=dst, buf=buf:
                                              e.dma_start(dst, buf[:]),
                                              [c_qe.ev(nev + 1)])
                P.emit("sync", lambda e: e.nop(), [d.ev() for d in d_o])
                P.check(all_cts); P.run_block(nc, "a")

        if "b1" in phases:
            P = Plan()
            with ExitStack() as SS:
                sqd_sb = SS.enter_context(nc.sbuf_tensor("sqd_sb", [128, ND * 128], F32))
                ebm = SS.enter_context(nc.sbuf_tensor("ebm", [128, ND * 128], F32))
                ebt = [SS.enter_context(nc.sbuf_tensor(f"ebt{i}", [128, ND * 128], BF16))
                       for i in range(4)]
                kbuf = SS.enter_context(nc.sbuf_tensor("kbuf", [128, S], BF16))
                vbuf = SS.enter_context(nc.sbuf_tensor("vbuf", [128, S], BF16))
                qbuf = [SS.enter_context(nc.sbuf_tensor(f"qbuf{i}", [128, 512], BF16))
                        for i in range(4)]
                ex_sb = [SS.enter_context(nc.sbuf_tensor(f"ex{i}", [128, 512], BF16))
                         for i in range(2)]
                p_sb = [SS.enter_context(nc.sbuf_tensor(f"p_sb{i}", [128, 512], BF16))
                        for i in range(4)]
                l_sb = SS.enter_context(nc.sbuf_tensor("l_sb", [1, 1024], F32))
                lr_sb = SS.enter_context(nc.sbuf_tensor("lr_sb", [1, 1024], F32))
                lb_sb = SS.enter_context(nc.sbuf_tensor("lb_sb", [1, 1024], BF16))
                rb_sb = SS.enter_context(nc.sbuf_tensor("rb_sb", [128, 512], F32))
                o_sb = [SS.enter_context(nc.sbuf_tensor(f"o_sb{i}", [128, 512], BF16))
                        for i in range(2)]
                ps_o = [SS.enter_context(nc.psum_tensor(f"ps_o{i}", [128, 512], F32))
                        for i in range(2)]
                ps_sc = [SS.enter_context(nc.psum_tensor(f"ps_sc{i}", [128, 512], F32))
                         for i in range(2)]
                ps_l = [SS.enter_context(nc.psum_tensor(f"ps_l{i}", [1, 512], F32))
                        for i in range(2)]
                ps_rb = SS.enter_context(nc.psum_tensor("ps_rb", [128, 512], F32))
                d_sqd = DST(mksem("b1_sqd"))
                d_k = DST(mksem("b1_k"))
                d_v = DST(mksem("b1_v"))
                d_q = [DST(mksem(f"b1_q{i}")) for i in range(4)]
                c_eb = mksem("b1_eb")
                c_ebs = mksem("b1_ebs")
                c_qk = mksem("b1_qk")
                c_ex = mksem("b1_ex")
                c_pm = mksem("b1_pm")
                c_lm = mksem("b1_lm")
                c_pv = mksem("b1_pv")
                c_tl = mksem("b1_tl")
                c_bc = mksem("b1_bc")
                d_os = [DST(mksem(f"b1_os{i}")) for i in range(2)]

                d_sqd.emit(P, "sync", lambda e: e.dma_start(sqd_sb[:], sqd_d))
                import os as _os
                _NG = int(_os.environ.get("B1_G", G))
                _NRR = int(_os.environ.get("B1_R", NR))
                for g in range(_NG):
                    qk_g0 = c_qk.n
                    pv_g0 = c_pv.n
                    pm_g0 = c_pm.n
                    for j in range(4):
                        h = 4 * g + j
                        waits = [d_sqd.ev()]
                        if h >= 1:
                            waits.append(c_ebs.ev(h))
                        P.emit("vector", lambda e, h=h:
                               e.tensor_scalar_mul(ebm[:], sqd_sb[:], -SLOPES[h]),
                               waits, (c_eb.tick(),))
                        waits = [c_eb.ev(h + 1)]
                        if g >= 1:
                            waits.append(c_pm.ev(pm_g0))
                        P.emit("scalar", lambda e, j=j:
                               e.activation(ebt[j][:], ebm[:], AF.Exp),
                               waits, (c_ebs.tick(),))
                    for r in range(_NRR):
                        nu = 4 * r + 4
                        waits = [c_qk.ev(qk_g0)] if g >= 1 else []
                        d_k.emit(P, "sync", lambda e, g=g, r=r:
                                 e.dma_start(kbuf[:, 512 * r:512 * (r + 1)],
                                             kT_sp[g][:, 512 * r:512 * (r + 1)]),
                                 waits)
                        for uu in range(4):
                            u = 4 * r + uu
                            waits = [c_pv.ev(pv_g0)] if g >= 1 else []
                            d_v.emit(P, "sync", lambda e, g=g, u=u:
                                     e.dma_start(vbuf[:, 128 * u:128 * (u + 1)],
                                                 v_sp[u][:, 128 * g:128 * (g + 1)]),
                                     waits)
                        for j in range(4):
                            h = 4 * g + j
                            waits = [c_qk.ev()] if (r >= 1 or g >= 1) else []
                            d_q[j].emit(P, "sync", lambda e, h=h, r=r, j=j:
                                        e.dma_start(qbuf[j][:],
                                                    q_sp[h][:, 512 * r:512 * (r + 1)]),
                                        waits)
                        ld_nows = (d_k.ev(), d_v.ev(), d_q[0].ev(), d_q[1].ev(),
                                   d_q[2].ev(), d_q[3].ev())
                        for pr in range(2):
                            ex_base = c_ex.n
                            pm_base = c_pm.n
                            lm_base = c_lm.n
                            for u in range(nu):
                                for j in range(2):
                                    rp = 2 * pr + j
                                    waits = list(ld_nows)
                                    if u >= 1:
                                        waits.append(c_ex.ev(ex_base + 2 * u))
                                    else:
                                        if c_ex.n > 0:
                                            waits.append(c_ex.ev())
                                        if c_tl.n > 0:
                                            waits.append(c_tl.ev())
                                    P.emit("tensor", lambda e, u=u, j=j, rp=rp:
                                           e.matmul(ps_sc[j][:],
                                                    kbuf[:, 128 * u:128 * (u + 1)],
                                                    qbuf[rp][:], start=True, stop=True,
                                                    skip_group_check=True),
                                           waits, (c_qk.tick(),))
                                qk_now = c_qk.n
                                for j in range(2):
                                    waits = [c_qk.ev(qk_now)]
                                    if u >= 1:
                                        waits.append(c_pm.ev(pm_base + 2 * u))
                                    elif c_pm.n > 0:
                                        waits.append(c_pm.ev())
                                    P.emit("scalar", lambda e, j=j:
                                           e.activation(ex_sb[j][:], ps_sc[j][:],
                                                        AF.Exp, bias=CSHIFT),
                                           waits, (c_ex.tick(),))
                                ex_now = c_ex.n
                                for j in range(2):
                                    rp = 2 * pr + j
                                    pi = 2 * (u % 2) + j
                                    col = 128 * (4 * r - u + 3)
                                    waits = [c_ex.ev(ex_now),
                                             c_ebs.ev(4 * g + rp + 1)]
                                    if u >= 2:
                                        waits.append(c_lm.ev(lm_base + 2 * (u - 1)))
                                    elif c_lm.n > 0:
                                        waits.append(c_lm.ev())
                                    P.emit("vector", lambda e, j=j, pi=pi, col=col, rp=rp:
                                           e.tensor_tensor(p_sb[pi][:], ex_sb[j][:],
                                                           ebt[rp][:, col:col + 512],
                                                           ALU.mult),
                                           waits, (c_pm.tick(),))
                                pm_now = c_pm.n
                                for j in range(2):
                                    pi = 2 * (u % 2) + j
                                    waits = [c_pm.ev(pm_now)]
                                    inc = [c_pv.tick()] if u == nu - 1 else ()
                                    P.emit("tensor", lambda e, u=u, j=j, pi=pi, nu=nu:
                                           e.matmul(ps_o[j][:],
                                                    vbuf[:, 128 * u:128 * (u + 1)],
                                                    p_sb[pi][:],
                                                    start=(u == 0), stop=(u == nu - 1),
                                                    skip_group_check=True),
                                           waits, inc)
                                    P.emit("tensor", lambda e, u=u, j=j, pi=pi, nu=nu:
                                           e.matmul(ps_l[j][:], ones_col[:],
                                                    p_sb[pi][:],
                                                    start=(u == 0), stop=(u == nu - 1),
                                                    skip_group_check=True),
                                           (), (c_lm.tick(),))
                            for j in range(2):
                                h = 4 * g + 2 * pr + j
                                P.emit("vector", lambda e, j=j:
                                       e.tensor_copy(l_sb[:, 512 * j:512 * (j + 1)],
                                                     ps_l[j][:]),
                                       [c_lm.ev()], (c_tl.tick(),))
                                P.emit("vector", lambda e, j=j:
                                       e.reciprocal(lr_sb[:, 512 * j:512 * (j + 1)],
                                                    l_sb[:, 512 * j:512 * (j + 1)]),
                                       [c_tl.ev()], (c_tl.tick(),))
                                P.emit("vector", lambda e, j=j:
                                       e.tensor_copy(lb_sb[:, 512 * j:512 * (j + 1)],
                                                     lr_sb[:, 512 * j:512 * (j + 1)]),
                                       [c_tl.ev()], (c_tl.tick(),))
                                P.emit("tensor", lambda e, j=j:
                                       e.matmul(ps_rb[:], ones_row[:],
                                                lb_sb[:, 512 * j:512 * (j + 1)],
                                                start=True, stop=True,
                                                skip_group_check=True),
                                       [c_tl.ev()], (c_bc.tick(),))
                                P.emit("vector", lambda e:
                                       e.tensor_copy(rb_sb[:], ps_rb[:]),
                                       [c_bc.ev()], (c_tl.tick(),))
                                waits = [c_tl.ev(), c_pv.ev(), c_lm.ev(), c_bc.ev()]
                                if d_os[j].n > 0:
                                    waits.append(d_os[j].ev())
                                P.emit("vector", lambda e, j=j:
                                       e.tensor_tensor(o_sb[j][:], ps_o[j][:],
                                                       rb_sb[:], ALU.mult),
                                       waits, (c_tl.tick(),))
                                d_os[j].emit(P, "scalar",
                                             lambda e, h=h, r=r, j=j:
                                             e.dma_start(
                                                 oT_sp[h][:, 512 * r:512 * (r + 1)],
                                                 o_sb[j][:]),
                                             [c_tl.ev()])
                P.emit("sync", lambda e: e.nop(), [d_os[0].ev(), d_os[1].ev()])
                P.check(all_cts); P.run_block(nc, "b1")

        if "b2" in phases:
            P = Plan()
            with ExitStack() as SS:
                oT_buf = SS.enter_context(nc.sbuf_tensor("oT_buf", [128, 16, 512], BF16))
                wo_buf = [SS.enter_context(nc.sbuf_tensor(f"wo_buf{i}", [128, 512], BF16))
                          for i in range(2)]
                xb2 = [SS.enter_context(nc.sbuf_tensor(f"xb2_{i}", [128, D], F32))
                       for i in range(2)]
                ao4 = [SS.enter_context(nc.sbuf_tensor(f"ao{i}", [128, D], F32))
                       for i in range(4)]
                t2_sb = SS.enter_context(nc.sbuf_tensor("t2_sb", [128, D], F32))
                xm_sb = [SS.enter_context(nc.sbuf_tensor(f"xmb{i}", [128, D], F32))
                         for i in range(2)]
                ln2_sb = SS.enter_context(nc.sbuf_tensor("ln2_sb", [128, D], BF16))
                l2t_sb = [SS.enter_context(nc.sbuf_tensor(f"l2t{i}", [128, 128], BF16))
                          for i in range(2)]
                bns2 = SS.enter_context(nc.sbuf_tensor("bns2", [128, 24], F32))
                mv2 = SS.enter_context(nc.sbuf_tensor("mv2", [128, 2], F32))
                sdr2 = SS.enter_context(nc.sbuf_tensor("sdr2", [128, 2], F32))
                ps_b2 = [SS.enter_context(nc.psum_tensor(f"ps_b2_{i}", [128, 512], F32))
                         for i in range(4)]
                ps_t2 = [SS.enter_context(nc.psum_tensor(f"ps_t2_{i}", [128, 128], BF16))
                         for i in range(2)]
                c_lo = mksem("b2_lo")
                d_wo = [DST(mksem(f"b2_wo{i}")) for i in range(2)]
                d_x2 = [DST(mksem(f"b2_x{i}")) for i in range(2)]
                c_m = mksem("b2_m")
                c_e = mksem("b2_e")
                c_v = mksem("b2_v")
                c_s = mksem("b2_s")
                c_t = mksem("b2_t")
                c_te = mksem("b2_te")
                d_sxm = [DST(mksem(f"b2_sxm{i}")) for i in range(2)]
                d_sl2 = [DST(mksem(f"b2_sl2{i}")) for i in range(2)]

                add2_ev = {}
                for r in range(NR):
                    m_r0 = c_m.n
                    for of in range(16):
                        waits = [c_m.ev()] if (r >= 1 and of == 0) else []
                        P.emit("sync", lambda e, of=of, r=r:
                               e.dma_start(oT_buf[:, of, :],
                                           oT_sp[of][:, 512 * r:512 * (r + 1)]),
                               waits, (c_lo.tick(16),))
                    for q4 in range(4):
                        for of in range(16):
                            gi = 16 * q4 + of
                            waits = []
                            if gi >= 2:
                                waits.append(c_m.ev(m_r0 + 4 * (gi - 1)))
                            d_wo[gi % 2].emit(
                                P, "sync", lambda e, q4=q4, of=of, gi=gi:
                                e.dma_start(wo_buf[gi % 2][:],
                                            wob_sp[of][:, 512 * q4:512 * (q4 + 1)]),
                                waits)
                            for tt in range(4):
                                waits = [c_lo.ev(16 * 16 * (r + 1)),
                                         d_wo[gi % 2].ev()]
                                if of == 0:
                                    prev_evac = 16 * r + 4 * q4 - 4 + tt + 1
                                    if prev_evac > 0:
                                        waits.append(c_e.ev(prev_evac))
                                P.emit("tensor", lambda e, q4=q4, of=of, tt=tt, gi=gi:
                                       e.matmul(ps_b2[tt][:],
                                                oT_buf[:, of, 128 * tt:128 * (tt + 1)],
                                                wo_buf[gi % 2][:],
                                                start=(of == 0), stop=(of == 15),
                                                skip_group_check=True),
                                       waits, (c_m.tick(),))
                        for tt in range(4):
                            P.emit("vector", lambda e, q4=q4, tt=tt:
                                   e.tensor_copy(ao4[tt][:, 512 * q4:512 * (q4 + 1)],
                                                 ps_b2[tt][:]),
                                   [c_m.ev()], (c_e.tick(),))
                    for tt in range(4):
                        t = 4 * r + tt
                        lwaits = []
                        if t >= 2:
                            lwaits.append((c_v.sem, add2_ev[t - 2]))
                        d_x2[t % 2].emit(P, "sync", lambda e, t=t:
                                         e.dma_start(xb2[t % 2][:],
                                                     x_d[128 * t:128 * (t + 1), :]),
                                         lwaits)
                        for i in range(4):
                            waits = [c_e.ev()]
                            if i > 0:
                                waits = [c_v.ev()]
                            P.emit("vector", lambda e, tt=tt, i=i:
                                   e.bn_stats(bns2[:, 6 * i:6 * (i + 1)],
                                              ao4[tt][:, 512 * i:512 * (i + 1)]),
                                   waits, (c_v.tick(),))
                        P.emit("vector", lambda e: e.bn_aggr(mv2[:], bns2[:]),
                               [c_v.ev()], (c_v.tick(),))
                        P.emit("scalar", lambda e:
                               e.activation(sdr2[:, 0:1], mv2[:, 1:2], AF.Sqrt,
                                            bias=LN_EPS),
                               [c_v.ev()], (c_s.tick(),))
                        P.emit("vector", lambda e:
                               e.reciprocal(sdr2[:, 1:2], sdr2[:, 0:1]),
                               [c_s.ev()], (c_v.tick(),))
                        P.emit("vector", lambda e, tt=tt:
                               e.tensor_scalar(t2_sb[:], ao4[tt][:],
                                               mv2[:, 0:1], sdr2[:, 1:2],
                                               ALU.subtract, ALU.mult),
                               [c_v.ev()], (c_v.tick(),))
                        waits = [c_v.ev(), d_x2[t % 2].ev()]
                        if t >= 2:
                            waits.append(d_sxm[t % 2].ev())
                        P.emit("vector", lambda e, t=t, tt=tt:
                               e.tensor_tensor(xm_sb[t % 2][:], t2_sb[:],
                                               xb2[t % 2][:], ALU.add),
                               waits, (c_v.tick(),))
                        add2_ev[t] = c_v.n
                        d_sxm[t % 2].emit(P, "scalar", lambda e, t=t:
                                          e.dma_start(xm_sp[128 * t:128 * (t + 1), :],
                                                      xm_sb[t % 2][:]),
                                          [c_v.ev()])
                        for i in range(4):
                            P.emit("vector", lambda e, t=t, i=i:
                                   e.bn_stats(bns2[:, 6 * i:6 * (i + 1)],
                                              xm_sb[t % 2][:, 512 * i:512 * (i + 1)]),
                                   [c_v.ev()] if i == 0 else (), (c_v.tick(),))
                        P.emit("vector", lambda e: e.bn_aggr(mv2[:], bns2[:]),
                               [c_v.ev()], (c_v.tick(),))
                        P.emit("scalar", lambda e:
                               e.activation(sdr2[:, 0:1], mv2[:, 1:2], AF.Sqrt,
                                            bias=LN_EPS),
                               [c_v.ev()], (c_s.tick(),))
                        P.emit("vector", lambda e:
                               e.reciprocal(sdr2[:, 1:2], sdr2[:, 0:1]),
                               [c_s.ev()], (c_v.tick(),))
                        P.emit("vector", lambda e, t=t:
                               e.tensor_scalar(ln2_sb[:], xm_sb[t % 2][:],
                                               mv2[:, 0:1], sdr2[:, 1:2],
                                               ALU.subtract, ALU.mult),
                               [c_v.ev()], (c_v.tick(),))
                        for dc in range(16):
                            idx = c_t.n + 1
                            waits = [c_v.ev()]
                            if idx > 2:
                                waits.append(c_te.ev(idx - 2))
                            P.emit("tensor", lambda e, dc=dc:
                                   e.transpose(ps_t2[dc % 2][:],
                                               ln2_sb[:, 128 * dc:128 * (dc + 1)],
                                               id_b[:]),
                                   waits, (c_t.tick(),))
                            waits = [c_t.ev()]
                            if d_sl2[dc % 2].n > 0:
                                waits.append(d_sl2[dc % 2].ev())
                            P.emit("scalar", lambda e, dc=dc:
                                   e.copy(l2t_sb[dc % 2][:], ps_t2[dc % 2][:]),
                                   waits, (c_te.tick(),))
                            d_sl2[dc % 2].emit(
                                P, "scalar", lambda e, dc=dc, t=t:
                                e.dma_start(l2t_sp[dc][:, 128 * t:128 * (t + 1)],
                                            l2t_sb[dc % 2][:]),
                                [c_te.ev()])
                P.emit("sync", lambda e: e.nop(),
                       [d_sxm[0].ev(), d_sxm[1].ev(), d_sl2[0].ev(), d_sl2[1].ev()])
                P.check(all_cts); P.run_block(nc, "b2")

        if "c" in phases:
            P = Plan()
            with ExitStack() as SS:
                l2T_buf = SS.enter_context(nc.sbuf_tensor("l2T_buf", [128, 16, 512], BF16))
                wgu_buf = [SS.enter_context(nc.sbuf_tensor(f"wgub{i}", [128, 16, 512], BF16))
                           for i in range(2)]
                wp_buf = [SS.enter_context(nc.sbuf_tensor(f"wpb{i}", [128, 512], BF16))
                          for i in range(2)]
                h_st = SS.enter_context(nc.sbuf_tensor("h_st", [128, 64, 512], BF16))
                silu_sb = SS.enter_context(nc.sbuf_tensor("silu_sb", [128, 512], F32))
                hsq_sb = SS.enter_context(nc.sbuf_tensor("hsq_sb", [128, 512], BF16))
                ss_sb = SS.enter_context(nc.sbuf_tensor("ss_sb", [128, 512], F32))
                rs_sb = SS.enter_context(nc.sbuf_tensor("rs_sb", [128, 8], F32))
                ffn4 = [SS.enter_context(nc.sbuf_tensor(f"ffn{i}", [128, D], F32))
                        for i in range(4)]
                xmC = [SS.enter_context(nc.sbuf_tensor(f"xmc{i}", [128, D], F32))
                       for i in range(2)]
                t2C = SS.enter_context(nc.sbuf_tensor("t2C", [128, D], F32))
                y_sb = [SS.enter_context(nc.sbuf_tensor(f"y_sb{i}", [128, D], F32))
                        for i in range(2)]
                bnsC = SS.enter_context(nc.sbuf_tensor("bnsC", [128, 24], F32))
                mvC = SS.enter_context(nc.sbuf_tensor("mvC", [128, 2], F32))
                sdrC = SS.enter_context(nc.sbuf_tensor("sdrC", [128, 2], F32))
                ps_W = [SS.enter_context(nc.psum_tensor(f"ps_W{i}", [128, 512], F32))
                        for i in range(4)]
                ps_ss = SS.enter_context(nc.psum_tensor("ps_ss", [128, 512], F32))
                ps_tc = SS.enter_context(nc.psum_tensor("ps_tc", [128, 128], F32))
                c_l2 = mksem("c_l2")
                c_wg = mksem("c_wg")
                d_wp = [DST(mksem(f"c_wp{i}")) for i in range(2)]
                d_xm = [DST(mksem(f"c_xm{i}")) for i in range(2)]
                c_g = mksem("c_g")
                c_sl = mksem("c_sl")
                c_h = mksem("c_h")
                c_hq = mksem("c_hq")
                c_ssm = mksem("c_ssm")
                c_sse = mksem("c_sse")
                c_tc = mksem("c_tc")
                c_rs = mksem("c_rs")
                c_rsv = mksem("c_rsv")
                c_2m = mksem("c_2m")
                c_2e = mksem("c_2e")
                c_v = mksem("c_v")
                c_s = mksem("c_s")

                add_ev = {}
                import os as _os
                _NRC = int(_os.environ.get("C_R", NR))
                _CST = _os.environ.get("C_STAGE", "full")
                for r in range(_NRC):
                    g_r0 = c_g.n
                    m2_r0 = c_2m.n
                    for dc in range(16):
                        waits = []
                        if r >= 1 and dc == 0:
                            waits.append(c_g.ev())
                        P.emit("sync", lambda e, dc=dc, r=r:
                               e.dma_start(l2T_buf[:, dc, :],
                                           l2t_sp[dc][:, 512 * r:512 * (r + 1)]),
                               waits, (c_l2.tick(16),))
                    _NHCQ = int(_os.environ.get("C_HCQ", 16))
                    for hcq in range(_NHCQ):
                        for part in range(2):
                            quad = hcq if part == 0 else hcq + 16
                            waits = []
                            if r > 0 or hcq > 0:
                                waits.append(c_g.ev(c_g.n))
                            for dcq in range(16):
                                P.emit("sync", lambda e, quad=quad, part=part, dcq=dcq:
                                       e.dma_start(wgu_buf[part][:, dcq, :],
                                                   wgu_sp[quad, dcq]),
                                       waits if dcq == 0 else (),
                                       (c_wg.tick(16),))
                        for sub in range(4):
                            hc = 4 * hcq + sub
                            for part in range(2):
                                pw = ps_W[2 * part + (sub % 2)]
                                waits = [c_wg.ev(), c_l2.ev(16 * 16 * (r + 1))]
                                need = 64 * r + hc - 1
                                if need > 0:
                                    waits.append((c_sl if part == 0 else c_h).ev(need))
                                if hcq == 0 and sub == 0 and r >= 1:
                                    waits.append(c_2e.ev(16 * r))

                                def mkg(e, part=part, sub=sub, pw=pw):
                                    for dc in range(16):
                                        inst = e.matmul(
                                            pw[:], wgu_buf[part][:, dc, 128 * sub:128 * (sub + 1)],
                                            l2T_buf[:, dc, :],
                                            start=(dc == 0), stop=(dc == 15),
                                            skip_group_check=True)
                                    return inst
                                P.emit("tensor", mkg, waits, (c_g.tick(),))
                        for sub in range(4):
                            hc = 4 * hcq + sub
                            pg = ps_W[0 + (sub % 2)]
                            pu = ps_W[2 + (sub % 2)]
                            gate_ev = g_r0 + 8 * hcq + 2 * sub + 1
                            up_ev = g_r0 + 8 * hcq + 2 * sub + 2
                            waits = [c_g.ev(gate_ev)]
                            if c_h.n > 0:
                                waits.append(c_h.ev())
                            P.emit("scalar", lambda e, pg=pg:
                                   e.activation(silu_sb[:], pg[:], AF.Silu),
                                   waits, (c_sl.tick(),))
                            hwaits = [c_sl.ev(), c_g.ev(up_ev)]
                            if r >= 1:
                                hwaits.append(c_2m.ev(1024 * r))
                            P.emit("vector", lambda e, pu=pu, hc=hc:
                                   e.tensor_tensor(h_st[:, hc, :], silu_sb[:],
                                                   pu[:], ALU.mult),
                                   hwaits, (c_h.tick(),))
                            waits = [c_h.ev()]
                            if c_ssm.n > 0:
                                waits.append(c_ssm.ev())
                            P.emit("scalar", lambda e, hc=hc:
                                   e.activation(hsq_sb[:], h_st[:, hc, :], AF.Square),
                                   waits, (c_hq.tick(),))
                            if _CST != "c0":
                                sswaits = [c_hq.ev()]
                                if hc == 0 and r >= 1:
                                    sswaits.append(c_sse.ev(r))
                                P.emit("tensor", lambda e, hc=hc:
                                       e.matmul(ps_ss[:], ones_full[:], hsq_sb[:],
                                                start=(hc == 0), stop=(hc == 4 * _NHCQ - 1),
                                                skip_group_check=True),
                                       sswaits, (c_ssm.tick(),))
                    if _CST in ("c0",):
                        continue
                    sswaits = [c_ssm.ev()]
                    if r >= 1:
                        sswaits.append(c_tc.ev(4 * r))
                    P.emit("vector", lambda e:
                           e.tensor_copy(ss_sb[:], ps_ss[:]),
                           sswaits, (c_sse.tick(),))
                    for tt in range(4):
                        waits = [c_sse.ev()]
                        if tt > 0:
                            waits.append(c_rs.ev())
                        P.emit("tensor", lambda e, tt=tt:
                               e.transpose(ps_tc[:], ss_sb[:, 128 * tt:128 * (tt + 1)],
                                           id_f[:]),
                               waits, (c_tc.tick(),))
                        rwaits = [c_tc.ev()]
                        if r >= 1:
                            rwaits.append(c_2e.ev(16 * r))
                        P.emit("scalar", lambda e, tt=tt:
                               e.activation(rs_sb[:, 2 * tt:2 * tt + 1],
                                            ps_tc[:, 0:1], AF.Sqrt,
                                            bias=RMS_EPS, scale=f32(1.0 / FF)),
                               rwaits, (c_rs.tick(),))
                        P.emit("vector", lambda e, tt=tt:
                               e.reciprocal(rs_sb[:, 2 * tt + 1:2 * tt + 2],
                                            rs_sb[:, 2 * tt:2 * tt + 1]),
                               [c_rs.ev()], (c_rsv.tick(),))
                    for q4 in range(4 if _CST in ("c2", "full") else 0):
                        for c in range(64):
                            gi = 64 * q4 + c
                            waits = []
                            if gi >= 2:
                                waits.append(c_2m.ev(m2_r0 + 4 * (gi - 1)))
                            d_wp[gi % 2].emit(
                                P, "sync", lambda e, c=c, q4=q4, gi=gi:
                                e.dma_start(wp_buf[gi % 2][:],
                                            wp_sp[c][:, 512 * q4:512 * (q4 + 1)]),
                                waits)
                            for tt in range(4):
                                waits = [d_wp[gi % 2].ev()]
                                if c == 0:
                                    waits.append(c_h.ev(64 * (r + 1)))
                                    if q4 >= 1:
                                        waits.append(c_2e.ev(16 * r + 4 * (q4 - 1) + tt + 1))
                                P.emit("tensor", lambda e, c=c, tt=tt, gi=gi:
                                       e.matmul(ps_W[tt][:],
                                                h_st[:, c, 128 * tt:128 * (tt + 1)],
                                                wp_buf[gi % 2][:],
                                                start=(c == 0), stop=(c == 63),
                                                skip_group_check=True),
                                       waits, (c_2m.tick(),))
                        for tt in range(4):
                            P.emit("vector", lambda e, q4=q4, tt=tt:
                                   e.tensor_scalar_mul(
                                       ffn4[tt][:, 512 * q4:512 * (q4 + 1)],
                                       ps_W[tt][:], rs_sb[:, 2 * tt + 1:2 * tt + 2]),
                                   [c_2m.ev(), c_rsv.ev(4 * (r + 1))], (c_2e.tick(),))
                    for tt in range(4 if _CST == "full" else 0):
                        t = 4 * r + tt
                        lwaits = []
                        gt = 4 * r + tt
                        if gt >= 2:
                            lwaits.append((c_v.sem, add_ev[gt - 2]))
                        d_xm[tt % 2].emit(P, "sync", lambda e, t=t, tt=tt:
                                          e.dma_start(xmC[tt % 2][:],
                                                      xm_sp[128 * t:128 * (t + 1), :]),
                                          lwaits)
                        for i in range(4):
                            P.emit("vector", lambda e, tt=tt, i=i:
                                   e.bn_stats(bnsC[:, 6 * i:6 * (i + 1)],
                                              ffn4[tt][:, 512 * i:512 * (i + 1)]),
                                   [c_2e.ev()] if i == 0 else (), (c_v.tick(),))
                        P.emit("vector", lambda e: e.bn_aggr(mvC[:], bnsC[:]),
                               [c_v.ev()], (c_v.tick(),))
                        P.emit("scalar", lambda e:
                               e.activation(sdrC[:, 0:1], mvC[:, 1:2], AF.Sqrt,
                                            bias=LN_EPS),
                               [c_v.ev()], (c_s.tick(),))
                        P.emit("vector", lambda e:
                               e.reciprocal(sdrC[:, 1:2], sdrC[:, 0:1]),
                               [c_s.ev()], (c_v.tick(),))
                        P.emit("vector", lambda e, tt=tt:
                               e.tensor_scalar(t2C[:], ffn4[tt][:],
                                               mvC[:, 0:1], sdrC[:, 1:2],
                                               ALU.subtract, ALU.mult),
                               [c_v.ev()], (c_v.tick(),))
                        waits = [c_v.ev(), d_xm[tt % 2].ev()]
                        if d_y[tt % 2].n > 0:
                            waits.append(d_y[tt % 2].ev())
                        P.emit("vector", lambda e, tt=tt:
                               e.tensor_tensor(y_sb[tt % 2][:], t2C[:],
                                               xmC[tt % 2][:], ALU.add),
                               waits, (c_v.tick(),))
                        add_ev[gt] = c_v.n
                        d_y[tt % 2].emit(P, "scalar", lambda e, t=t, tt=tt:
                                         e.dma_start(y_d[128 * t:128 * (t + 1), :],
                                                     y_sb[tt % 2][:]),
                                         [c_v.ev()])
                P.check(all_cts); P.run_block(nc, "c")

            Pf = Plan()
            if d_y[0].n > 0 or d_y[1].n > 0:
                Pf.emit("sync", lambda e: e.nop(), [d_y[0].ev(), d_y[1].ev()])
            else:
                Pf.emit("sync", lambda e: e.nop())
            Pf.check(all_cts); Pf.run_block(nc, "fin")

    return nc


def _host_tables():
    i = np.arange(128)
    tables = {}
    pk = i[:, None]
    cols = np.arange(ND * 128)[None, :]
    w = cols // 128
    pq = cols % 128
    dd = 128 * (w - 3) + pq - pk
    valid = (w >= 3) & (dd >= 0)
    sq = np.where(valid, np.sqrt(np.maximum(dd, 0).astype(np.float64)), 1e30)
    tables["sqd"] = sq.astype(np.float32)
    tables["idb"] = np.eye(128, dtype=ml_dtypes.bfloat16)
    tables["idf"] = np.eye(128, dtype=np.float32)
    bones = np.zeros((128, 2), dtype=ml_dtypes.bfloat16)
    bones[0:64, 0] = 1
    bones[64:128, 1] = 1
    tables["bonesc"] = bones
    sel = np.zeros((2, 128), dtype=ml_dtypes.bfloat16)
    sel[0, 0:64] = 1
    sel[1, 64:128] = 1
    tables["selc"] = sel
    return tables


def kernel(L0_attn_x, L0_attn_Wq, L0_attn_Wk, L0_attn_Wv, L0_attn_Wo,
           L0_ffn_w_gate_up, L0_ffn_w_proj):
    x = np.asarray(L0_attn_x, dtype=np.float32)
    wqT = np.ascontiguousarray(np.asarray(L0_attn_Wq, dtype=np.float32).T)
    wkT = np.ascontiguousarray(np.asarray(L0_attn_Wk, dtype=np.float32).T)
    wvT = np.ascontiguousarray(np.asarray(L0_attn_Wv, dtype=np.float32).T)
    woT = np.ascontiguousarray(np.asarray(L0_attn_Wo, dtype=np.float32).T)
    wguT = np.ascontiguousarray(np.asarray(L0_ffn_w_gate_up, dtype=np.float32).T)
    wpT = np.ascontiguousarray(np.asarray(L0_ffn_w_proj, dtype=np.float32).T)
    tables = _host_tables()
    nc = build_program()
    in_maps = []
    for b in range(8):
        m = {"x": np.ascontiguousarray(x[b]),
             "wqT": wqT, "wkT": wkT, "wvT": wvT, "woT": woT,
             "wguT": wguT, "wpT": wpT}
        m.update(tables)
        in_maps.append(m)
    res = run_bass_kernel_spmd(nc, in_maps, core_ids=list(range(8)))
    out = np.stack([res.results[b]["y"] for b in range(8)], axis=0)
    return out


# version 02
# speedup vs baseline: 1.1566x; 1.1566x over previous
import math
import numpy as np
import ml_dtypes
import concourse.bass as bass
import concourse.mybir as mybir
from concourse.bass_utils import run_bass_kernel_spmd
from contextlib import ExitStack

F32 = mybir.dt.float32
BF16 = mybir.dt.bfloat16
AF = mybir.ActivationFunctionType
ALU = mybir.AluOpType

f32 = np.float32

S = 4096
D = 2048
NT = S // 128
NR = S // 512
H = 16
G = 4
FF = 8192
ATT_SCALE = f32(1.0 / math.sqrt(128.0))
CSHIFT = f32(-8.0)
LN_EPS = f32(1e-5)
RMS_EPS = f32(np.finfo(np.float32).eps)
QTHR = 0.501953125
ND = 35
SLOPES = [float((2.0 ** (-0.5)) ** (i + 1)) for i in range(16)]

ALL_PHASES = ("p0", "a", "b1", "b2", "c")


class Plan:
    ENGINES = ("sync", "gpsimd", "vector", "scalar", "tensor")

    def __init__(self):
        self.items = {e: [] for e in self.ENGINES}

    def emit(self, eng, fn, waits=(), incs=()):
        self.items[eng].append((tuple(waits), fn, tuple(incs)))

    def check(self, cts):
        final = {c.sem: c.n for c in cts}
        mult = {c.sem: (16 if c.dma else 1) for c in cts}
        for e, items in self.items.items():
            for i, (waits, fn, incs) in enumerate(items):
                for sem, v in waits:
                    assert v <= final.get(sem, 0), \
                        (e, i, sem.name, v, final.get(sem, 0))
                    assert v % mult.get(sem, 1) == 0, (e, i, sem.name, v)

    def run_block(self, nc, name):
        import os as _os
        drop = set((_os.environ.get("C_DROP", "") or "").split(","))
        if name in ("c", "fin") and drop & set(self.ENGINES):
            for e in drop & set(self.ENGINES):
                self.items[e] = []
        def runner(items):
            def body(eng):
                for waits, fn, incs in items:
                    for sem, v in waits:
                        if v > 0:
                            eng.wait_ge(sem, v)
                    inst = fn(eng)
                    for sem, k in incs:
                        inst.then_inc(sem, k)
            return body
        with nc.Block(name) as blk:
            blk.sync(runner(self.items["sync"]))
            blk.gpsimd(runner(self.items["gpsimd"]))
            blk.vector(runner(self.items["vector"]))
            blk.scalar(runner(self.items["scalar"]))
            blk.tensor(runner(self.items["tensor"]))


class DST:

    def __init__(self, ct):
        self.ct = ct

    def emit(self, P, eng, fn, waits=()):
        w = list(waits)
        if self.ct.n > 0:
            w.append(self.ct.ev())
        P.emit(eng, fn, w, (self.ct.tick(16),))

    def ev(self, n=None):
        return self.ct.ev(n)

    @property
    def n(self):
        return self.ct.n


class CT:
    def __init__(self, sem):
        self.sem = sem
        self.n = 0
        self.dma = None

    def tick(self, k=1):
        if self.dma is None:
            self.dma = (k == 16)
        self.n += k
        return (self.sem, k)

    def ev(self, n=None):
        return (self.sem, self.n if n is None else n)


def build_program(phases=ALL_PHASES, dump=False):
    nc = bass.Bass("TRN2", target_bir_lowering=False, debug=False)
    skind = "ExternalOutput" if dump else "Internal"

    x_d = nc.dram_tensor("x", [S, D], F32, kind="ExternalInput").ap()
    wqT_d = nc.dram_tensor("wqT", [D, D], F32, kind="ExternalInput").ap()
    wkT_d = nc.dram_tensor("wkT", [D, 512], F32, kind="ExternalInput").ap()
    wvT_d = nc.dram_tensor("wvT", [D, 512], F32, kind="ExternalInput").ap()
    woT_d = nc.dram_tensor("woT", [D, D], F32, kind="ExternalInput").ap()
    wguT_d = nc.dram_tensor("wguT", [D, 2 * FF], F32, kind="ExternalInput").ap()
    wpT_d = nc.dram_tensor("wpT", [FF, D], F32, kind="ExternalInput").ap()
    sqd_d = nc.dram_tensor("sqd", [128, ND * 128], F32, kind="ExternalInput").ap()
    idb_d = nc.dram_tensor("idb", [128, 128], BF16, kind="ExternalInput").ap()
    idf_d = nc.dram_tensor("idf", [128, 128], F32, kind="ExternalInput").ap()
    bones_d = nc.dram_tensor("bonesc", [128, 2], BF16, kind="ExternalInput").ap()
    sel_d = nc.dram_tensor("selc", [2, 128], BF16, kind="ExternalInput").ap()
    y_d = nc.dram_tensor("y", [S, D], F32, kind="ExternalOutput").ap()

    q_sp = nc.dram_tensor("q_sp", [H, 128, S], BF16, kind=skind).ap()
    kT_sp = nc.dram_tensor("kT_sp", [G, 128, S], BF16, kind=skind).ap()
    v_sp = nc.dram_tensor("v_sp", [NT, 128, 512], BF16, kind=skind).ap()
    oT_sp = nc.dram_tensor("oT_sp", [H, 128, S], BF16, kind=skind).ap()
    l2t_sp = nc.dram_tensor("l2t_sp", [16, 128, S], BF16, kind=skind).ap()
    xm_sp = nc.dram_tensor("xm_sp", [S, D], F32, kind=skind).ap()
    wob_sp = nc.dram_tensor("wob_sp", [16, 128, D], BF16, kind=skind).ap()
    wgu_sp = nc.dram_tensor("wgu_sp", [32, 16, 128, 512], BF16, kind=skind).ap()
    wp_sp = nc.dram_tensor("wp_sp", [64, 128, D], BF16, kind=skind).ap()

    with ExitStack() as PS:
        id_b = PS.enter_context(nc.sbuf_tensor("id_b", [128, 128], BF16))
        id_f = PS.enter_context(nc.sbuf_tensor("id_f", [128, 128], F32))
        ones_col = PS.enter_context(nc.sbuf_tensor("ones_col", [128, 1], BF16))
        ones_row = PS.enter_context(nc.sbuf_tensor("ones_row", [1, 128], BF16))
        ones_full = PS.enter_context(nc.sbuf_tensor("ones_full", [128, 128], BF16))
        bones = PS.enter_context(nc.sbuf_tensor("bones", [128, 2], BF16))
        sel = PS.enter_context(nc.sbuf_tensor("sel", [2, 128], BF16))

        all_cts = []

        def mksem(name):
            c = CT(PS.enter_context(nc.semaphore(name)))
            all_cts.append(c)
            return c

        d_y = [DST(mksem(f"y_st{i}")) for i in range(2)]

        if "p0" in phases:
            P = Plan()
            with ExitStack() as SS:
                wo_tmp = [SS.enter_context(nc.sbuf_tensor(f"wotmp{i}", [128, D], BF16))
                          for i in range(2)]
                c_ld = mksem("p0a_ld")
                d_cast = [DST(mksem(f"p0a_cast{i}")) for i in range(2)]
                d_st = [DST(mksem(f"p0a_st{i}")) for i in range(2)]
                P.emit("sync", lambda e: e.dma_start(id_b[:], idb_d), (), (c_ld.tick(16),))
                P.emit("sync", lambda e: e.dma_start(id_f[:], idf_d), (), (c_ld.tick(16),))
                P.emit("vector", lambda e: e.memset(ones_col[:], 1.0))
                P.emit("vector", lambda e: e.memset(ones_row[:], 1.0))
                P.emit("vector", lambda e: e.memset(ones_full[:], 1.0))
                P.emit("sync", lambda e: e.dma_start(bones[:], bones_d), (),
                       (c_ld.tick(16),))
                P.emit("sync", lambda e: e.dma_start(sel[:], sel_d), (),
                       (c_ld.tick(16),))
                for of in range(16):
                    waits = [d_st[of % 2].ev()] if of >= 2 else []
                    d_cast[of % 2].emit(
                        P, "gpsimd",
                        lambda e, of=of: e.dma_start(
                            wo_tmp[of % 2][:], woT_d[128 * of:128 * (of + 1), :]),
                        waits)
                    d_st[of % 2].emit(
                        P, "scalar",
                        lambda e, of=of: e.dma_start(wob_sp[of], wo_tmp[of % 2][:]),
                        [d_cast[of % 2].ev()])
                P.emit("sync", lambda e: e.nop(),
                       [d_st[0].ev(), d_st[1].ev(), c_ld.ev()])
                P.check(all_cts); P.run_block(nc, "p0a")

        if "p0" in phases:
            P = Plan()
            with ExitStack() as SS:
                slab = [SS.enter_context(nc.sbuf_tensor(f"slab{i}", [128, FF], BF16))
                        for i in range(2)]
                aw_sb = SS.enter_context(nc.sbuf_tensor("aw_sb", [128, FF], BF16))
                sg_sb = SS.enter_context(nc.sbuf_tensor("sg_sb", [128, FF], BF16))
                sc_sb = [SS.enter_context(nc.sbuf_tensor(f"sc{i}", [2, 512], BF16))
                         for i in range(2)]
                thr_sb = [SS.enter_context(nc.sbuf_tensor(f"thr{i}", [128, 512], F32))
                          for i in range(2)]
                msk_sb = [SS.enter_context(nc.sbuf_tensor(f"msk{i}", [128, 512], BF16))
                          for i in range(2)]
                t1_sb = [SS.enter_context(nc.sbuf_tensor(f"t1_{i}", [128, 512], BF16))
                         for i in range(2)]
                wh_sb = [SS.enter_context(nc.sbuf_tensor(f"wh{i}", [128, 512], BF16))
                         for i in range(2)]
                ps_s = [SS.enter_context(nc.psum_tensor(f"ps_s{i}", [2, 512], F32))
                        for i in range(2)]
                ps_b = [SS.enter_context(nc.psum_tensor(f"ps_b{i}", [128, 512], F32))
                        for i in range(2)]
                d_ld = [DST(mksem(f"q_ld{i}")) for i in range(2)]
                c_abs = mksem("q_abs")
                c_sum = mksem("q_sum")
                c_scl = mksem("q_scl")
                c_bc = mksem("q_bc")
                c_thr = mksem("q_thr")
                c_msk = mksem("q_msk")
                c_t1 = mksem("q_t1")
                c_wh = mksem("q_wh")
                d_sp = [DST(mksem(f"q_sp{i}")) for i in range(2)]

                jobs = []
                for dc in range(16):
                    for hs in range(2):
                        src = wguT_d[128 * dc:128 * (dc + 1), FF * hs:FF * (hs + 1)]
                        dsts = [wgu_sp[16 * hs + w, dc] for w in range(16)]
                        jobs.append((src, FF, dsts))
                for c in range(64):
                    src = wpT_d[128 * c:128 * (c + 1), :]
                    dsts = [wp_sp[c][:, 512 * w:512 * (w + 1)] for w in range(4)]
                    jobs.append((src, D, dsts))
                cum = [0]
                for _, _, dsts in jobs:
                    cum.append(cum[-1] + len(dsts))

                nwin = 0
                for ji, (src, width, dsts) in enumerate(jobs):
                    waits = []
                    if ji >= 2:
                        waits.append(c_abs.ev(2 * (ji - 1)))
                    d_ld[ji % 2].emit(P, "gpsimd",
                                      lambda e, s_=src, w_=width, ji=ji:
                                      e.dma_start(slab[ji % 2][:, 0:w_], s_),
                                      waits)
                    waits = [d_ld[ji % 2].ev()]
                    if ji >= 1:
                        waits.append(c_msk.ev(cum[ji]))
                        waits.append(c_t1.ev(cum[ji]))
                    P.emit("scalar", lambda e, w_=width, ji=ji:
                           e.activation(aw_sb[:, 0:w_], slab[ji % 2][:, 0:w_], AF.Abs),
                           waits, (c_abs.tick(),))
                    P.emit("scalar", lambda e, w_=width, ji=ji:
                           e.activation(sg_sb[:, 0:w_], slab[ji % 2][:, 0:w_], AF.Sign),
                           (), (c_abs.tick(),))
                    for w, dst in enumerate(dsts):
                        nwin += 1
                        p = nwin % 2
                        waits = [c_abs.ev(2 * ji + 1)]
                        if nwin > 2:
                            waits.append(c_scl.ev(nwin - 2))
                        P.emit("tensor", lambda e, w=w, p=p:
                               e.matmul(ps_s[p][:], bones[:],
                                        aw_sb[:, 512 * w:512 * (w + 1)],
                                        start=True, stop=True, skip_group_check=True),
                               waits, (c_sum.tick(),))
                        waits = [c_sum.ev(nwin)]
                        if nwin > 2:
                            waits.append(c_bc.ev(nwin - 2))
                        P.emit("vector", lambda e, p=p:
                               e.tensor_scalar(sc_sb[p][:], ps_s[p][:],
                                               1.0 / 64.0, 1e-8, ALU.mult, ALU.max),
                               waits, (c_scl.tick(),))
                        waits = [c_scl.ev(nwin)]
                        if nwin > 2:
                            waits.append(c_wh.ev(nwin - 2))
                        P.emit("tensor", lambda e, p=p:
                               e.matmul(ps_b[p][:], sel[:], sc_sb[p][:],
                                        start=True, stop=True, skip_group_check=True),
                               waits, (c_bc.tick(),))
                        waits = [c_bc.ev(nwin)]
                        if nwin > 2:
                            waits.append(c_msk.ev(nwin - 2))
                        P.emit("vector", lambda e, p=p:
                               e.tensor_scalar_mul(thr_sb[p][:], ps_b[p][:], QTHR),
                               waits, (c_thr.tick(),))
                        P.emit("vector", lambda e, p=p, w=w:
                               e.tensor_tensor(msk_sb[p][:],
                                               aw_sb[:, 512 * w:512 * (w + 1)],
                                               thr_sb[p][:], ALU.is_gt),
                               [c_thr.ev(nwin)], (c_msk.tick(),))
                        P.emit("vector", lambda e, p=p, w=w:
                               e.tensor_tensor(t1_sb[p][:], msk_sb[p][:],
                                               sg_sb[:, 512 * w:512 * (w + 1)],
                                               ALU.mult),
                               [c_msk.ev(nwin)], (c_t1.tick(),))
                        waits = [c_t1.ev(nwin)]
                        if nwin > 2:
                            waits.append(d_sp[nwin % 2].ev())
                        P.emit("vector", lambda e, p=p, nw=nwin:
                               e.tensor_tensor(wh_sb[nw % 2][:], t1_sb[p][:],
                                               ps_b[p][:], ALU.mult),
                               waits, (c_wh.tick(),))
                        d_sp[nwin % 2].emit(P, "scalar",
                                            lambda e, dst=dst, nw=nwin:
                                            e.dma_start(dst, wh_sb[nw % 2][:]),
                                            [c_wh.ev(nwin)])
                P.emit("sync", lambda e: e.nop(), [d_sp[0].ev(), d_sp[1].ev()])
                P.check(all_cts); P.run_block(nc, "p0b")

        if "a" in phases:
            P = Plan()
            with ExitStack() as SS:
                wq_sb = SS.enter_context(nc.sbuf_tensor("wq_sb", [128, 16, D], BF16))
                wk_sb = SS.enter_context(nc.sbuf_tensor("wk_sb", [128, 16, 512], BF16))
                wv_sb = SS.enter_context(nc.sbuf_tensor("wv_sb", [128, 16, 512], BF16))
                xb = [SS.enter_context(nc.sbuf_tensor(f"xa{i}", [128, D], F32))
                      for i in range(2)]
                lnb = [SS.enter_context(nc.sbuf_tensor(f"lnb{i}", [128, D], BF16))
                       for i in range(2)]
                lxT = SS.enter_context(nc.sbuf_tensor("lxT", [128, 16, 512], BF16))
                ev_sb = [SS.enter_context(nc.sbuf_tensor(f"aev{i}", [128, 512], BF16))
                         for i in range(4)]
                bns = SS.enter_context(nc.sbuf_tensor("bns", [128, 24], F32))
                mv = SS.enter_context(nc.sbuf_tensor("mv", [128, 2], F32))
                sdr = SS.enter_context(nc.sbuf_tensor("sdr", [128, 2], F32))
                ps_tr = [SS.enter_context(nc.psum_tensor(f"ps_tr{i}", [128, 128], BF16))
                         for i in range(2)]
                ps_mm = [SS.enter_context(nc.psum_tensor(f"ps_mm{i}", [128, 512], F32))
                         for i in range(2)]
                c_w = mksem("a_w")
                d_x = [DST(mksem(f"a_x{i}")) for i in range(2)]
                c_v = mksem("a_v")
                c_s = mksem("a_s")
                c_t = mksem("a_t")
                c_te = mksem("a_te")
                c_m = mksem("a_m")
                c_qe = mksem("a_qe")
                d_o = [DST(mksem(f"a_o{i}")) for i in range(4)]

                for wsb, wd in [(wq_sb, wqT_d), (wk_sb, wkT_d), (wv_sb, wvT_d)]:
                    for dc in range(16):
                        P.emit("gpsimd", lambda e, wsb=wsb, wd=wd, dc=dc:
                               e.dma_start(wsb[:, dc, :], wd[128 * dc:128 * (dc + 1), :]),
                               (), (c_w.tick(16),))
                VPT = 7
                for t in range(NT):
                    r, tt = t // 4, t % 4
                    waits = [c_v.ev(VPT * (t - 1))] if t >= 2 else []
                    d_x[t % 2].emit(P, "sync", lambda e, t=t:
                                    e.dma_start(xb[t % 2][:],
                                                x_d[128 * t:128 * (t + 1), :]),
                                    waits)
                    for i in range(4):
                        waits = [d_x[t % 2].ev()] if i == 0 else []
                        P.emit("vector", lambda e, t=t, i=i:
                               e.bn_stats(bns[:, 6 * i:6 * (i + 1)],
                                          xb[t % 2][:, 512 * i:512 * (i + 1)]),
                               waits, (c_v.tick(),))
                    P.emit("vector", lambda e: e.bn_aggr(mv[:], bns[:]),
                           [c_v.ev()], (c_v.tick(),))
                    P.emit("scalar", lambda e:
                           e.activation(sdr[:, 0:1], mv[:, 1:2], AF.Sqrt, bias=LN_EPS),
                           [c_v.ev()], (c_s.tick(),))
                    P.emit("vector", lambda e: e.reciprocal(sdr[:, 1:2], sdr[:, 0:1]),
                           [c_s.ev()], (c_v.tick(),))
                    waits = [c_v.ev()]
                    if t >= 2:
                        waits.append(c_t.ev(16 * (t - 1)))
                    P.emit("vector", lambda e, t=t:
                           e.tensor_scalar(lnb[t % 2][:], xb[t % 2][:],
                                           mv[:, 0:1], sdr[:, 1:2],
                                           ALU.subtract, ALU.mult),
                           waits, (c_v.tick(),))
                    for dc in range(16):
                        idx = 16 * t + dc + 1
                        waits = [c_v.ev(VPT * (t + 1))]
                        if idx > 2:
                            waits.append(c_te.ev(idx - 2))
                        P.emit("tensor", lambda e, t=t, dc=dc:
                               e.transpose(ps_tr[dc % 2][:],
                                           lnb[t % 2][:, 128 * dc:128 * (dc + 1)],
                                           id_b[:]),
                               waits, (c_t.tick(),))
                        ewaits = [c_t.ev(idx), c_m.ev(24 * r)]
                        P.emit("scalar", lambda e, tt=tt, dc=dc:
                               e.copy(lxT[:, dc, 128 * tt:128 * (tt + 1)],
                                      ps_tr[dc % 2][:]),
                               ewaits, (c_te.tick(),))
                    if tt == 3:
                        for i in range(24):
                            nev = 24 * r + i
                            p = ps_mm[nev % 2]
                            waits = [c_w.ev(16 * 48), c_te.ev(64 * (r + 1))]
                            if nev >= 2:
                                waits.append(c_qe.ev(nev - 1))

                            def mk(e, i=i, p=p):
                                if i < 16:
                                    for dc in range(16):
                                        inst = e.matmul(
                                            p[:], wq_sb[:, dc, 128 * i:128 * (i + 1)],
                                            lxT[:, dc, :],
                                            start=(dc == 0), stop=(dc == 15),
                                            skip_group_check=True)
                                elif i < 20:
                                    kf = i - 16
                                    for dc in range(16):
                                        inst = e.matmul(
                                            p[:], wk_sb[:, dc, 128 * kf:128 * (kf + 1)],
                                            lxT[:, dc, :],
                                            start=(dc == 0), stop=(dc == 15),
                                            skip_group_check=True)
                                else:
                                    tv = i - 20
                                    for dc in range(16):
                                        inst = e.matmul(
                                            p[:], lxT[:, dc, 128 * tv:128 * (tv + 1)],
                                            wv_sb[:, dc, :],
                                            start=(dc == 0), stop=(dc == 15),
                                            skip_group_check=True)
                                return inst
                            P.emit("tensor", mk, waits, (c_m.tick(),))
                            buf = ev_sb[nev % 4]
                            waits = [c_m.ev(nev + 1)]
                            if nev >= 4:
                                waits.append(d_o[nev % 4].ev())
                            if i < 16:
                                P.emit("scalar", lambda e, buf=buf, p=p:
                                       e.activation(buf[:], p[:], AF.Copy,
                                                    bias=0.0, scale=ATT_SCALE),
                                       waits, (c_qe.tick(),))
                            else:
                                P.emit("scalar", lambda e, buf=buf, p=p:
                                       e.copy(buf[:], p[:]),
                                       waits, (c_qe.tick(),))
                            if i < 16:
                                dst = q_sp[i][:, 512 * r:512 * (r + 1)]
                            elif i < 20:
                                dst = kT_sp[i - 16][:, 512 * r:512 * (r + 1)]
                            else:
                                dst = v_sp[4 * r + (i - 20)]
                            d_o[nev % 4].emit(P, "scalar",
                                              lambda e, dst=dst, buf=buf:
                                              e.dma_start(dst, buf[:]),
                                              [c_qe.ev(nev + 1)])
                P.emit("sync", lambda e: e.nop(), [d.ev() for d in d_o])
                P.check(all_cts); P.run_block(nc, "a")

        if "b1" in phases:
            P = Plan()
            with ExitStack() as SS:
                sqd_sb = SS.enter_context(nc.sbuf_tensor("sqd_sb", [128, ND * 128], F32))
                ebm = SS.enter_context(nc.sbuf_tensor("ebm", [128, ND * 128], F32))
                ebt = [SS.enter_context(nc.sbuf_tensor(f"ebt{i}", [128, ND * 128], BF16))
                       for i in range(4)]
                kbuf = SS.enter_context(nc.sbuf_tensor("kbuf", [128, S], BF16))
                vbuf = SS.enter_context(nc.sbuf_tensor("vbuf", [128, S], BF16))
                qbuf = [[SS.enter_context(nc.sbuf_tensor(f"qbuf{i}_{p}", [128, 512], BF16))
                         for p in range(2)] for i in range(4)]
                ex_sb = [SS.enter_context(nc.sbuf_tensor(f"ex{i}", [128, 512], BF16))
                         for i in range(2)]
                p_sb = [SS.enter_context(nc.sbuf_tensor(f"p_sb{i}", [128, 512], BF16))
                        for i in range(4)]
                l_sb = SS.enter_context(nc.sbuf_tensor("l_sb", [1, 1024], F32))
                lr_sb = SS.enter_context(nc.sbuf_tensor("lr_sb", [1, 1024], F32))
                lb_sb = SS.enter_context(nc.sbuf_tensor("lb_sb", [1, 1024], BF16))
                rb_sb = SS.enter_context(nc.sbuf_tensor("rb_sb", [128, 512], F32))
                o_sb = [SS.enter_context(nc.sbuf_tensor(f"o_sb{i}", [128, 512], BF16))
                        for i in range(2)]
                ps_o = [SS.enter_context(nc.psum_tensor(f"ps_o{i}", [128, 512], F32))
                        for i in range(2)]
                ps_sc = [SS.enter_context(nc.psum_tensor(f"ps_sc{i}", [128, 512], F32))
                         for i in range(4)]
                ps_l = [SS.enter_context(nc.psum_tensor(f"ps_l{i}", [1, 512], F32))
                        for i in range(2)]
                d_sqd = DST(mksem("b1_sqd"))
                d_k = DST(mksem("b1_k"))
                d_v = DST(mksem("b1_v"))
                d_q = [DST(mksem(f"b1_q{i}")) for i in range(4)]
                c_eb = mksem("b1_eb")
                c_ebs = mksem("b1_ebs")
                c_qk = mksem("b1_qk")
                c_ex = mksem("b1_ex")
                c_pm = mksem("b1_pm")
                c_lm = mksem("b1_lm")
                c_pv = mksem("b1_pv")
                c_tl = mksem("b1_tl")
                c_bc = mksem("b1_bc")
                d_os = [DST(mksem(f"b1_os{i}")) for i in range(2)]

                d_sqd.emit(P, "sync", lambda e: e.dma_start(sqd_sb[:], sqd_d))
                scfree = [None] * 4
                exfree = [None] * 2
                pfree = [None] * 4
                ofree = [None] * 2
                lfree = [None] * 2
                qk_end = {}
                import os as _os
                _NG = int(_os.environ.get("B1_G", G))
                _NRR = int(_os.environ.get("B1_R", NR))
                for g in range(_NG):
                    qk_g0 = c_qk.n
                    pv_g0 = c_pv.n
                    pm_g0 = c_pm.n
                    for j in range(4):
                        h = 4 * g + j
                        waits = [d_sqd.ev()]
                        if h >= 1:
                            waits.append(c_ebs.ev(h))
                        P.emit("vector", lambda e, h=h:
                               e.tensor_scalar_mul(ebm[:], sqd_sb[:], -SLOPES[h]),
                               waits, (c_eb.tick(),))
                        waits = [c_eb.ev(h + 1)]
                        if g >= 1:
                            waits.append(c_pm.ev(pm_g0))
                        P.emit("scalar", lambda e, j=j:
                               e.activation(ebt[j][:], ebm[:], AF.Exp),
                               waits, (c_ebs.tick(),))
                    for r in range(_NRR):
                        nu = 4 * r + 4
                        waits = [c_qk.ev(qk_g0)] if g >= 1 else []
                        d_k.emit(P, "sync", lambda e, g=g, r=r:
                                 e.dma_start(kbuf[:, 512 * r:512 * (r + 1)],
                                             kT_sp[g][:, 512 * r:512 * (r + 1)]),
                                 waits)
                        for uu in range(4):
                            u = 4 * r + uu
                            waits = [c_pv.ev(pv_g0)] if g >= 1 else []
                            d_v.emit(P, "sync", lambda e, g=g, u=u:
                                     e.dma_start(vbuf[:, 128 * u:128 * (u + 1)],
                                                 v_sp[u][:, 128 * g:128 * (g + 1)]),
                                     waits)
                        for j in range(4):
                            h = 4 * g + j
                            waits = []
                            if (g, r - 2) in qk_end:
                                waits.append(c_qk.ev(qk_end[(g, r - 2)]))
                            elif g >= 1 and r < 2:
                                waits.append(c_qk.ev(qk_g0))
                            d_q[j].emit(P, "sync", lambda e, h=h, r=r, j=j:
                                        e.dma_start(qbuf[j][r % 2][:],
                                                    q_sp[h][:, 512 * r:512 * (r + 1)]),
                                        waits)
                        ld_nows = (d_k.ev(), d_v.ev(), d_q[0].ev(), d_q[1].ev(),
                                   d_q[2].ev(), d_q[3].ev())
                        for pr in range(2):
                            pvl = []
                            for u in range(nu):
                                for j in range(2):
                                    rp = 2 * pr + j
                                    b = 2 * pr + j
                                    waits = []
                                    if pr == 0 and u == 0 and j == 0:
                                        waits += list(ld_nows)
                                    if scfree[b] is not None:
                                        waits.append(scfree[b])
                                    P.emit("tensor", lambda e, u=u, rp=rp, b=b, r=r:
                                           e.matmul(ps_sc[b][:],
                                                    kbuf[:, 128 * u:128 * (u + 1)],
                                                    qbuf[rp][r % 2][:],
                                                    start=True, stop=True,
                                                    skip_group_check=True),
                                           waits, (c_qk.tick(),))
                                    qk_this = c_qk.n
                                    waits = [c_qk.ev(qk_this)]
                                    if exfree[j] is not None:
                                        waits.append(exfree[j])
                                    P.emit("scalar", lambda e, j=j, b=b:
                                           e.activation(ex_sb[j][:], ps_sc[b][:],
                                                        AF.Exp, bias=CSHIFT),
                                           waits, (c_ex.tick(),))
                                    scfree[b] = c_ex.ev()
                                    pi = 2 * (u % 2) + j
                                    col = 128 * (4 * r - u + 3)
                                    waits = [c_ex.ev(), c_ebs.ev(4 * g + rp + 1)]
                                    if pfree[pi] is not None:
                                        waits.append(pfree[pi])
                                    P.emit("vector",
                                           lambda e, j=j, pi=pi, col=col, rp=rp:
                                           e.tensor_tensor(p_sb[pi][:], ex_sb[j][:],
                                                           ebt[rp][:, col:col + 512],
                                                           ALU.mult),
                                           waits, (c_pm.tick(),))
                                    exfree[j] = c_pm.ev()
                                    pm_this = c_pm.ev()
                                    def mk_pvl(u=u, j=j, pi=pi, nu=nu, pm_this=pm_this):
                                        waits = [pm_this]
                                        if u == 0 and ofree[j] is not None:
                                            waits.append(ofree[j])
                                        inc = [c_pv.tick()] if u == nu - 1 else ()
                                        P.emit("tensor",
                                               lambda e, u=u, j=j, pi=pi, nu=nu:
                                               e.matmul(ps_o[j][:],
                                                        vbuf[:, 128 * u:128 * (u + 1)],
                                                        p_sb[pi][:],
                                                        start=(u == 0),
                                                        stop=(u == nu - 1),
                                                        skip_group_check=True),
                                               waits, inc)
                                        waits = []
                                        if u == 0 and lfree[j] is not None:
                                            waits.append(lfree[j])
                                        P.emit("tensor",
                                               lambda e, u=u, j=j, pi=pi, nu=nu:
                                               e.matmul(ps_l[j][:], ones_col[:],
                                                        p_sb[pi][:],
                                                        start=(u == 0),
                                                        stop=(u == nu - 1),
                                                        skip_group_check=True),
                                               waits, (c_lm.tick(),))
                                        pfree[pi] = c_lm.ev()
                                    pvl.append(mk_pvl)
                                if u >= 1:
                                    pvl.pop(0)()
                                    pvl.pop(0)()
                            while pvl:
                                pvl.pop(0)()
                            for j in range(2):
                                h = 4 * g + 2 * pr + j
                                b = 2 * pr + j
                                P.emit("vector", lambda e, j=j:
                                       e.tensor_copy(l_sb[:, 512 * j:512 * (j + 1)],
                                                     ps_l[j][:]),
                                       [c_lm.ev()], (c_tl.tick(),))
                                lfree[j] = c_tl.ev()
                                P.emit("vector", lambda e, j=j:
                                       e.reciprocal(lr_sb[:, 512 * j:512 * (j + 1)],
                                                    l_sb[:, 512 * j:512 * (j + 1)]),
                                       [c_tl.ev()], (c_tl.tick(),))
                                P.emit("vector", lambda e, j=j:
                                       e.tensor_copy(lb_sb[:, 512 * j:512 * (j + 1)],
                                                     lr_sb[:, 512 * j:512 * (j + 1)]),
                                       [c_tl.ev()], (c_tl.tick(),))
                                waits = [c_tl.ev(), scfree[b]]
                                P.emit("tensor", lambda e, j=j, b=b:
                                       e.matmul(ps_sc[b][:], ones_row[:],
                                                lb_sb[:, 512 * j:512 * (j + 1)],
                                                start=True, stop=True,
                                                skip_group_check=True),
                                       waits, (c_bc.tick(),))
                                P.emit("vector", lambda e, b=b:
                                       e.tensor_copy(rb_sb[:], ps_sc[b][:]),
                                       [c_bc.ev()], (c_tl.tick(),))
                                scfree[b] = c_tl.ev()
                                waits = [c_tl.ev(), c_pv.ev()]
                                if d_os[j].n > 0:
                                    waits.append(d_os[j].ev())
                                P.emit("vector", lambda e, j=j:
                                       e.tensor_tensor(o_sb[j][:], ps_o[j][:],
                                                       rb_sb[:], ALU.mult),
                                       waits, (c_tl.tick(),))
                                ofree[j] = c_tl.ev()
                                d_os[j].emit(P, "scalar",
                                             lambda e, h=h, r=r, j=j:
                                             e.dma_start(
                                                 oT_sp[h][:, 512 * r:512 * (r + 1)],
                                                 o_sb[j][:]),
                                             [c_tl.ev()])
                        qk_end[(g, r)] = c_qk.n
                P.emit("sync", lambda e: e.nop(), [d_os[0].ev(), d_os[1].ev()])
                P.check(all_cts); P.run_block(nc, "b1")


        if "b2" in phases:
            P = Plan()
            with ExitStack() as SS:
                oT_buf = SS.enter_context(nc.sbuf_tensor("oT_buf", [128, 16, 512], BF16))
                wo_buf = [SS.enter_context(nc.sbuf_tensor(f"wo_buf{i}", [128, 512], BF16))
                          for i in range(2)]
                xb2 = [SS.enter_context(nc.sbuf_tensor(f"xb2_{i}", [128, D], F32))
                       for i in range(2)]
                ao4 = [SS.enter_context(nc.sbuf_tensor(f"ao{i}", [128, D], F32))
                       for i in range(4)]
                t2_sb = SS.enter_context(nc.sbuf_tensor("t2_sb", [128, D], F32))
                xm_sb = [SS.enter_context(nc.sbuf_tensor(f"xmb{i}", [128, D], F32))
                         for i in range(2)]
                ln2_sb = SS.enter_context(nc.sbuf_tensor("ln2_sb", [128, D], BF16))
                l2t_sb = [SS.enter_context(nc.sbuf_tensor(f"l2t{i}", [128, 128], BF16))
                          for i in range(2)]
                bns2 = SS.enter_context(nc.sbuf_tensor("bns2", [128, 24], F32))
                mv2 = SS.enter_context(nc.sbuf_tensor("mv2", [128, 2], F32))
                sdr2 = SS.enter_context(nc.sbuf_tensor("sdr2", [128, 2], F32))
                ps_b2 = [SS.enter_context(nc.psum_tensor(f"ps_b2_{i}", [128, 512], F32))
                         for i in range(4)]
                ps_t2 = [SS.enter_context(nc.psum_tensor(f"ps_t2_{i}", [128, 128], BF16))
                         for i in range(2)]
                c_lo = mksem("b2_lo")
                d_wo = [DST(mksem(f"b2_wo{i}")) for i in range(2)]
                d_x2 = [DST(mksem(f"b2_x{i}")) for i in range(2)]
                c_m = mksem("b2_m")
                c_e = mksem("b2_e")
                c_v = mksem("b2_v")
                c_s = mksem("b2_s")
                c_t = mksem("b2_t")
                c_te = mksem("b2_te")
                d_sxm = [DST(mksem(f"b2_sxm{i}")) for i in range(2)]
                d_sl2 = [DST(mksem(f"b2_sl2{i}")) for i in range(2)]

                add2_ev = {}
                for r in range(NR):
                    m_r0 = c_m.n
                    for of in range(16):
                        waits = [c_m.ev()] if (r >= 1 and of == 0) else []
                        P.emit("sync", lambda e, of=of, r=r:
                               e.dma_start(oT_buf[:, of, :],
                                           oT_sp[of][:, 512 * r:512 * (r + 1)]),
                               waits, (c_lo.tick(16),))
                    for q4 in range(4):
                        for of in range(16):
                            gi = 16 * q4 + of
                            waits = []
                            if gi >= 2:
                                waits.append(c_m.ev(m_r0 + 4 * (gi - 1)))
                            d_wo[gi % 2].emit(
                                P, "sync", lambda e, q4=q4, of=of, gi=gi:
                                e.dma_start(wo_buf[gi % 2][:],
                                            wob_sp[of][:, 512 * q4:512 * (q4 + 1)]),
                                waits)
                            for tt in range(4):
                                waits = [c_lo.ev(16 * 16 * (r + 1)),
                                         d_wo[gi % 2].ev()]
                                if of == 0:
                                    prev_evac = 16 * r + 4 * q4 - 4 + tt + 1
                                    if prev_evac > 0:
                                        waits.append(c_e.ev(prev_evac))
                                P.emit("tensor", lambda e, q4=q4, of=of, tt=tt, gi=gi:
                                       e.matmul(ps_b2[tt][:],
                                                oT_buf[:, of, 128 * tt:128 * (tt + 1)],
                                                wo_buf[gi % 2][:],
                                                start=(of == 0), stop=(of == 15),
                                                skip_group_check=True),
                                       waits, (c_m.tick(),))
                        for tt in range(4):
                            P.emit("vector", lambda e, q4=q4, tt=tt:
                                   e.tensor_copy(ao4[tt][:, 512 * q4:512 * (q4 + 1)],
                                                 ps_b2[tt][:]),
                                   [c_m.ev()], (c_e.tick(),))
                    for tt in range(4):
                        t = 4 * r + tt
                        lwaits = []
                        if t >= 2:
                            lwaits.append((c_v.sem, add2_ev[t - 2]))
                        d_x2[t % 2].emit(P, "sync", lambda e, t=t:
                                         e.dma_start(xb2[t % 2][:],
                                                     x_d[128 * t:128 * (t + 1), :]),
                                         lwaits)
                        for i in range(4):
                            waits = [c_e.ev()]
                            if i > 0:
                                waits = [c_v.ev()]
                            P.emit("vector", lambda e, tt=tt, i=i:
                                   e.bn_stats(bns2[:, 6 * i:6 * (i + 1)],
                                              ao4[tt][:, 512 * i:512 * (i + 1)]),
                                   waits, (c_v.tick(),))
                        P.emit("vector", lambda e: e.bn_aggr(mv2[:], bns2[:]),
                               [c_v.ev()], (c_v.tick(),))
                        P.emit("scalar", lambda e:
                               e.activation(sdr2[:, 0:1], mv2[:, 1:2], AF.Sqrt,
                                            bias=LN_EPS),
                               [c_v.ev()], (c_s.tick(),))
                        P.emit("vector", lambda e:
                               e.reciprocal(sdr2[:, 1:2], sdr2[:, 0:1]),
                               [c_s.ev()], (c_v.tick(),))
                        P.emit("vector", lambda e, tt=tt:
                               e.tensor_scalar(t2_sb[:], ao4[tt][:],
                                               mv2[:, 0:1], sdr2[:, 1:2],
                                               ALU.subtract, ALU.mult),
                               [c_v.ev()], (c_v.tick(),))
                        waits = [c_v.ev(), d_x2[t % 2].ev()]
                        if t >= 2:
                            waits.append(d_sxm[t % 2].ev())
                        P.emit("vector", lambda e, t=t, tt=tt:
                               e.tensor_tensor(xm_sb[t % 2][:], t2_sb[:],
                                               xb2[t % 2][:], ALU.add),
                               waits, (c_v.tick(),))
                        add2_ev[t] = c_v.n
                        d_sxm[t % 2].emit(P, "scalar", lambda e, t=t:
                                          e.dma_start(xm_sp[128 * t:128 * (t + 1), :],
                                                      xm_sb[t % 2][:]),
                                          [c_v.ev()])
                        for i in range(4):
                            P.emit("vector", lambda e, t=t, i=i:
                                   e.bn_stats(bns2[:, 6 * i:6 * (i + 1)],
                                              xm_sb[t % 2][:, 512 * i:512 * (i + 1)]),
                                   [c_v.ev()] if i == 0 else (), (c_v.tick(),))
                        P.emit("vector", lambda e: e.bn_aggr(mv2[:], bns2[:]),
                               [c_v.ev()], (c_v.tick(),))
                        P.emit("scalar", lambda e:
                               e.activation(sdr2[:, 0:1], mv2[:, 1:2], AF.Sqrt,
                                            bias=LN_EPS),
                               [c_v.ev()], (c_s.tick(),))
                        P.emit("vector", lambda e:
                               e.reciprocal(sdr2[:, 1:2], sdr2[:, 0:1]),
                               [c_s.ev()], (c_v.tick(),))
                        P.emit("vector", lambda e, t=t:
                               e.tensor_scalar(ln2_sb[:], xm_sb[t % 2][:],
                                               mv2[:, 0:1], sdr2[:, 1:2],
                                               ALU.subtract, ALU.mult),
                               [c_v.ev()], (c_v.tick(),))
                        for dc in range(16):
                            idx = c_t.n + 1
                            waits = [c_v.ev()]
                            if idx > 2:
                                waits.append(c_te.ev(idx - 2))
                            P.emit("tensor", lambda e, dc=dc:
                                   e.transpose(ps_t2[dc % 2][:],
                                               ln2_sb[:, 128 * dc:128 * (dc + 1)],
                                               id_b[:]),
                                   waits, (c_t.tick(),))
                            waits = [c_t.ev()]
                            if d_sl2[dc % 2].n > 0:
                                waits.append(d_sl2[dc % 2].ev())
                            P.emit("scalar", lambda e, dc=dc:
                                   e.copy(l2t_sb[dc % 2][:], ps_t2[dc % 2][:]),
                                   waits, (c_te.tick(),))
                            d_sl2[dc % 2].emit(
                                P, "scalar", lambda e, dc=dc, t=t:
                                e.dma_start(l2t_sp[dc][:, 128 * t:128 * (t + 1)],
                                            l2t_sb[dc % 2][:]),
                                [c_te.ev()])
                P.emit("sync", lambda e: e.nop(),
                       [d_sxm[0].ev(), d_sxm[1].ev(), d_sl2[0].ev(), d_sl2[1].ev()])
                P.check(all_cts); P.run_block(nc, "b2")

        if "c" in phases:
            P = Plan()
            with ExitStack() as SS:
                l2T_buf = SS.enter_context(nc.sbuf_tensor("l2T_buf", [128, 16, 512], BF16))
                wgu_buf = [SS.enter_context(nc.sbuf_tensor(f"wgub{i}", [128, 16, 512], BF16))
                           for i in range(2)]
                wp_buf = [SS.enter_context(nc.sbuf_tensor(f"wpb{i}", [128, 512], BF16))
                          for i in range(4)]
                h_st = SS.enter_context(nc.sbuf_tensor("h_st", [128, 64, 512], BF16))
                silu_sb = SS.enter_context(nc.sbuf_tensor("silu_sb", [128, 512], F32))
                hsq_sb = SS.enter_context(nc.sbuf_tensor("hsq_sb", [128, 512], BF16))
                ss_sb = SS.enter_context(nc.sbuf_tensor("ss_sb", [128, 512], F32))
                rs_sb = SS.enter_context(nc.sbuf_tensor("rs_sb", [128, 8], F32))
                ffn4 = [SS.enter_context(nc.sbuf_tensor(f"ffn{i}", [128, D], F32))
                        for i in range(4)]
                xmC = [SS.enter_context(nc.sbuf_tensor(f"xmc{i}", [128, D], F32))
                       for i in range(2)]
                t2C = SS.enter_context(nc.sbuf_tensor("t2C", [128, D], F32))
                y_sb = [SS.enter_context(nc.sbuf_tensor(f"y_sb{i}", [128, D], F32))
                        for i in range(2)]
                bnsC = SS.enter_context(nc.sbuf_tensor("bnsC", [128, 24], F32))
                mvC = SS.enter_context(nc.sbuf_tensor("mvC", [128, 2], F32))
                sdrC = SS.enter_context(nc.sbuf_tensor("sdrC", [128, 2], F32))
                ps_W = [SS.enter_context(nc.psum_tensor(f"ps_W{i}", [128, 512], F32))
                        for i in range(4)]
                ps_ss = SS.enter_context(nc.psum_tensor("ps_ss", [128, 512], F32))
                ps_tc = SS.enter_context(nc.psum_tensor("ps_tc", [128, 128], F32))
                c_l2 = mksem("c_l2")
                c_wg = mksem("c_wg")
                d_wp = [DST(mksem(f"c_wp{i}")) for i in range(4)]
                d_xm = [DST(mksem(f"c_xm{i}")) for i in range(2)]
                c_g = mksem("c_g")
                c_sl = mksem("c_sl")
                c_h = mksem("c_h")
                c_hq = mksem("c_hq")
                c_ssm = mksem("c_ssm")
                c_sse = mksem("c_sse")
                c_tc = mksem("c_tc")
                c_rs = mksem("c_rs")
                c_rsv = mksem("c_rsv")
                c_2m = mksem("c_2m")
                c_2e = mksem("c_2e")
                c_v = mksem("c_v")
                c_s = mksem("c_s")

                add_ev = {}
                import os as _os
                _NRC = int(_os.environ.get("C_R", NR))
                _CST = _os.environ.get("C_STAGE", "full")
                for r in range(_NRC):
                    g_r0 = c_g.n
                    m2_r0 = c_2m.n
                    for dc in range(16):
                        waits = []
                        if r >= 1 and dc == 0:
                            waits.append(c_g.ev())
                        P.emit("sync", lambda e, dc=dc, r=r:
                               e.dma_start(l2T_buf[:, dc, :],
                                           l2t_sp[dc][:, 512 * r:512 * (r + 1)]),
                               waits, (c_l2.tick(16),))
                    _NHCQ = int(_os.environ.get("C_HCQ", 16))
                    for hcq in range(_NHCQ):
                        for part in range(2):
                            quad = hcq if part == 0 else hcq + 16
                            waits = []
                            if r > 0 or hcq > 0:
                                waits.append(c_g.ev(c_g.n))
                            for dcq in range(16):
                                P.emit("sync", lambda e, quad=quad, part=part, dcq=dcq:
                                       e.dma_start(wgu_buf[part][:, dcq, :],
                                                   wgu_sp[quad, dcq]),
                                       waits if dcq == 0 else (),
                                       (c_wg.tick(16),))
                        for sub in range(4):
                            hc = 4 * hcq + sub
                            for part in range(2):
                                pw = ps_W[2 * part + (sub % 2)]
                                waits = [c_wg.ev(), c_l2.ev(16 * 16 * (r + 1))]
                                need = 64 * r + hc - 1
                                if need > 0:
                                    waits.append((c_sl if part == 0 else c_h).ev(need))
                                if hcq == 0 and sub == 0 and r >= 1:
                                    waits.append(c_2e.ev(16 * r))

                                def mkg(e, part=part, sub=sub, pw=pw):
                                    for dc in range(16):
                                        inst = e.matmul(
                                            pw[:], wgu_buf[part][:, dc, 128 * sub:128 * (sub + 1)],
                                            l2T_buf[:, dc, :],
                                            start=(dc == 0), stop=(dc == 15),
                                            skip_group_check=True)
                                    return inst
                                P.emit("tensor", mkg, waits, (c_g.tick(),))
                        for sub in range(4):
                            hc = 4 * hcq + sub
                            pg = ps_W[0 + (sub % 2)]
                            pu = ps_W[2 + (sub % 2)]
                            gate_ev = g_r0 + 8 * hcq + 2 * sub + 1
                            up_ev = g_r0 + 8 * hcq + 2 * sub + 2
                            waits = [c_g.ev(gate_ev)]
                            if c_h.n > 0:
                                waits.append(c_h.ev())
                            P.emit("scalar", lambda e, pg=pg:
                                   e.activation(silu_sb[:], pg[:], AF.Silu),
                                   waits, (c_sl.tick(),))
                            hwaits = [c_sl.ev(), c_g.ev(up_ev)]
                            if r >= 1:
                                hwaits.append(c_2m.ev(1024 * r))
                            P.emit("vector", lambda e, pu=pu, hc=hc:
                                   e.tensor_tensor(h_st[:, hc, :], silu_sb[:],
                                                   pu[:], ALU.mult),
                                   hwaits, (c_h.tick(),))
                            waits = [c_h.ev()]
                            if c_ssm.n > 0:
                                waits.append(c_ssm.ev())
                            P.emit("scalar", lambda e, hc=hc:
                                   e.activation(hsq_sb[:], h_st[:, hc, :], AF.Square),
                                   waits, (c_hq.tick(),))
                            if _CST != "c0":
                                sswaits = [c_hq.ev()]
                                if hc == 0 and r >= 1:
                                    sswaits.append(c_sse.ev(r))
                                P.emit("tensor", lambda e, hc=hc:
                                       e.matmul(ps_ss[:], ones_full[:], hsq_sb[:],
                                                start=(hc == 0), stop=(hc == 4 * _NHCQ - 1),
                                                skip_group_check=True),
                                       sswaits, (c_ssm.tick(),))
                    if _CST in ("c0",):
                        continue
                    sswaits = [c_ssm.ev()]
                    if r >= 1:
                        sswaits.append(c_tc.ev(4 * r))
                    P.emit("vector", lambda e:
                           e.tensor_copy(ss_sb[:], ps_ss[:]),
                           sswaits, (c_sse.tick(),))
                    for tt in range(4):
                        waits = [c_sse.ev()]
                        if tt > 0:
                            waits.append(c_rs.ev())
                        P.emit("tensor", lambda e, tt=tt:
                               e.transpose(ps_tc[:], ss_sb[:, 128 * tt:128 * (tt + 1)],
                                           id_f[:]),
                               waits, (c_tc.tick(),))
                        rwaits = [c_tc.ev()]
                        if r >= 1:
                            rwaits.append(c_2e.ev(16 * r))
                        P.emit("scalar", lambda e, tt=tt:
                               e.activation(rs_sb[:, 2 * tt:2 * tt + 1],
                                            ps_tc[:, 0:1], AF.Sqrt,
                                            bias=RMS_EPS, scale=f32(1.0 / FF)),
                               rwaits, (c_rs.tick(),))
                        P.emit("vector", lambda e, tt=tt:
                               e.reciprocal(rs_sb[:, 2 * tt + 1:2 * tt + 2],
                                            rs_sb[:, 2 * tt:2 * tt + 1]),
                               [c_rs.ev()], (c_rsv.tick(),))
                    for q4 in range(4 if _CST in ("c2", "full") else 0):
                        for c in range(64):
                            gi = 64 * q4 + c
                            waits = []
                            if gi >= 4:
                                waits.append(c_2m.ev(m2_r0 + 4 * (gi - 3)))
                            d_wp[gi % 4].emit(
                                P, "sync", lambda e, c=c, q4=q4, gi=gi:
                                e.dma_start(wp_buf[gi % 4][:],
                                            wp_sp[c][:, 512 * q4:512 * (q4 + 1)]),
                                waits)
                            for tt in range(4):
                                waits = [d_wp[gi % 4].ev()]
                                if c == 0:
                                    waits.append(c_h.ev(64 * (r + 1)))
                                    if q4 >= 1:
                                        waits.append(c_2e.ev(16 * r + 4 * (q4 - 1) + tt + 1))
                                P.emit("tensor", lambda e, c=c, tt=tt, gi=gi:
                                       e.matmul(ps_W[tt][:],
                                                h_st[:, c, 128 * tt:128 * (tt + 1)],
                                                wp_buf[gi % 4][:],
                                                start=(c == 0), stop=(c == 63),
                                                skip_group_check=True),
                                       waits, (c_2m.tick(),))
                        for tt in range(4):
                            P.emit("vector", lambda e, q4=q4, tt=tt:
                                   e.tensor_scalar_mul(
                                       ffn4[tt][:, 512 * q4:512 * (q4 + 1)],
                                       ps_W[tt][:], rs_sb[:, 2 * tt + 1:2 * tt + 2]),
                                   [c_2m.ev(), c_rsv.ev(4 * (r + 1))], (c_2e.tick(),))
                    for tt in range(4 if _CST == "full" else 0):
                        t = 4 * r + tt
                        lwaits = []
                        gt = 4 * r + tt
                        if gt >= 2:
                            lwaits.append((c_v.sem, add_ev[gt - 2]))
                        d_xm[tt % 2].emit(P, "sync", lambda e, t=t, tt=tt:
                                          e.dma_start(xmC[tt % 2][:],
                                                      xm_sp[128 * t:128 * (t + 1), :]),
                                          lwaits)
                        for i in range(4):
                            P.emit("vector", lambda e, tt=tt, i=i:
                                   e.bn_stats(bnsC[:, 6 * i:6 * (i + 1)],
                                              ffn4[tt][:, 512 * i:512 * (i + 1)]),
                                   [c_2e.ev()] if i == 0 else (), (c_v.tick(),))
                        P.emit("vector", lambda e: e.bn_aggr(mvC[:], bnsC[:]),
                               [c_v.ev()], (c_v.tick(),))
                        P.emit("scalar", lambda e:
                               e.activation(sdrC[:, 0:1], mvC[:, 1:2], AF.Sqrt,
                                            bias=LN_EPS),
                               [c_v.ev()], (c_s.tick(),))
                        P.emit("vector", lambda e:
                               e.reciprocal(sdrC[:, 1:2], sdrC[:, 0:1]),
                               [c_s.ev()], (c_v.tick(),))
                        P.emit("vector", lambda e, tt=tt:
                               e.tensor_scalar(t2C[:], ffn4[tt][:],
                                               mvC[:, 0:1], sdrC[:, 1:2],
                                               ALU.subtract, ALU.mult),
                               [c_v.ev()], (c_v.tick(),))
                        waits = [c_v.ev(), d_xm[tt % 2].ev()]
                        if d_y[tt % 2].n > 0:
                            waits.append(d_y[tt % 2].ev())
                        P.emit("vector", lambda e, tt=tt:
                               e.tensor_tensor(y_sb[tt % 2][:], t2C[:],
                                               xmC[tt % 2][:], ALU.add),
                               waits, (c_v.tick(),))
                        add_ev[gt] = c_v.n
                        d_y[tt % 2].emit(P, "scalar", lambda e, t=t, tt=tt:
                                         e.dma_start(y_d[128 * t:128 * (t + 1), :],
                                                     y_sb[tt % 2][:]),
                                         [c_v.ev()])
                P.check(all_cts); P.run_block(nc, "c")

            Pf = Plan()
            if d_y[0].n > 0 or d_y[1].n > 0:
                Pf.emit("sync", lambda e: e.nop(), [d_y[0].ev(), d_y[1].ev()])
            else:
                Pf.emit("sync", lambda e: e.nop())
            Pf.check(all_cts); Pf.run_block(nc, "fin")

    return nc


def _host_tables():
    i = np.arange(128)
    tables = {}
    pk = i[:, None]
    cols = np.arange(ND * 128)[None, :]
    w = cols // 128
    pq = cols % 128
    dd = 128 * (w - 3) + pq - pk
    valid = (w >= 3) & (dd >= 0)
    sq = np.where(valid, np.sqrt(np.maximum(dd, 0).astype(np.float64)), 1e30)
    tables["sqd"] = sq.astype(np.float32)
    tables["idb"] = np.eye(128, dtype=ml_dtypes.bfloat16)
    tables["idf"] = np.eye(128, dtype=np.float32)
    bones = np.zeros((128, 2), dtype=ml_dtypes.bfloat16)
    bones[0:64, 0] = 1
    bones[64:128, 1] = 1
    tables["bonesc"] = bones
    sel = np.zeros((2, 128), dtype=ml_dtypes.bfloat16)
    sel[0, 0:64] = 1
    sel[1, 64:128] = 1
    tables["selc"] = sel
    return tables


def kernel(L0_attn_x, L0_attn_Wq, L0_attn_Wk, L0_attn_Wv, L0_attn_Wo,
           L0_ffn_w_gate_up, L0_ffn_w_proj):
    x = np.asarray(L0_attn_x, dtype=np.float32)
    wqT = np.ascontiguousarray(np.asarray(L0_attn_Wq, dtype=np.float32).T)
    wkT = np.ascontiguousarray(np.asarray(L0_attn_Wk, dtype=np.float32).T)
    wvT = np.ascontiguousarray(np.asarray(L0_attn_Wv, dtype=np.float32).T)
    woT = np.ascontiguousarray(np.asarray(L0_attn_Wo, dtype=np.float32).T)
    wguT = np.ascontiguousarray(np.asarray(L0_ffn_w_gate_up, dtype=np.float32).T)
    wpT = np.ascontiguousarray(np.asarray(L0_ffn_w_proj, dtype=np.float32).T)
    tables = _host_tables()
    nc = build_program()
    in_maps = []
    for b in range(8):
        m = {"x": np.ascontiguousarray(x[b]),
             "wqT": wqT, "wkT": wkT, "wvT": wvT, "woT": woT,
             "wguT": wguT, "wpT": wpT}
        m.update(tables)
        in_maps.append(m)
    res = run_bass_kernel_spmd(nc, in_maps, core_ids=list(range(8)))
    out = np.stack([res.results[b]["y"] for b in range(8)], axis=0)
    return out


# version 03 (final)
# speedup vs baseline: 1.4734x; 1.2740x over previous
import math
import numpy as np
import ml_dtypes
import concourse.bass as bass
import concourse.mybir as mybir
from concourse.bass_utils import run_bass_kernel_spmd
from contextlib import ExitStack

F32 = mybir.dt.float32
BF16 = mybir.dt.bfloat16
AF = mybir.ActivationFunctionType
ALU = mybir.AluOpType

f32 = np.float32

S = 4096
D = 2048
NT = S // 128
NR = S // 512
H = 16
G = 4
FF = 8192
ATT_SCALE = f32(1.0 / math.sqrt(128.0))
CSHIFT = f32(-8.0)
LN_EPS = f32(1e-5)
RMS_EPS = f32(np.finfo(np.float32).eps)
QTHR = 0.501953125
ND = 35
SLOPES = [float((2.0 ** (-0.5)) ** (i + 1)) for i in range(16)]

ALL_PHASES = ("p0", "a", "b1", "b2", "c")


class Plan:
    ENGINES = ("sync", "gpsimd", "vector", "scalar", "tensor")

    def __init__(self):
        self.items = {e: [] for e in self.ENGINES}

    def emit(self, eng, fn, waits=(), incs=()):
        self.items[eng].append((tuple(waits), fn, tuple(incs)))

    def check(self, cts):
        final = {c.sem: c.n for c in cts}
        mult = {c.sem: (16 if c.dma else 1) for c in cts}
        for e, items in self.items.items():
            for i, (waits, fn, incs) in enumerate(items):
                for sem, v in waits:
                    assert v <= final.get(sem, 0), \
                        (e, i, sem.name, v, final.get(sem, 0))
                    assert v % mult.get(sem, 1) == 0, (e, i, sem.name, v)

    def run_block(self, nc, name):
        import os as _os
        drop = set((_os.environ.get("C_DROP", "") or "").split(","))
        if name in ("c", "fin") and drop & set(self.ENGINES):
            for e in drop & set(self.ENGINES):
                self.items[e] = []
        def runner(items):
            def body(eng):
                for waits, fn, incs in items:
                    for sem, v in waits:
                        if v > 0:
                            eng.wait_ge(sem, v)
                    inst = fn(eng)
                    for sem, k in incs:
                        inst.then_inc(sem, k)
            return body
        with nc.Block(name) as blk:
            blk.sync(runner(self.items["sync"]))
            blk.gpsimd(runner(self.items["gpsimd"]))
            blk.vector(runner(self.items["vector"]))
            blk.scalar(runner(self.items["scalar"]))
            blk.tensor(runner(self.items["tensor"]))


class DST:

    def __init__(self, ct):
        self.ct = ct

    def emit(self, P, eng, fn, waits=()):
        w = list(waits)
        if self.ct.n > 0:
            w.append(self.ct.ev())
        P.emit(eng, fn, w, (self.ct.tick(16),))

    def ev(self, n=None):
        return self.ct.ev(n)

    @property
    def n(self):
        return self.ct.n


class CT:
    def __init__(self, sem):
        self.sem = sem
        self.n = 0
        self.dma = None

    def tick(self, k=1):
        if self.dma is None:
            self.dma = (k == 16)
        self.n += k
        return (self.sem, k)

    def ev(self, n=None):
        return (self.sem, self.n if n is None else n)


def build_program(phases=ALL_PHASES, dump=False):
    nc = bass.Bass("TRN2", target_bir_lowering=False, debug=False)
    skind = "ExternalOutput" if dump else "Internal"

    x_d = nc.dram_tensor("x", [S, D], F32, kind="ExternalInput").ap()
    wqT_d = nc.dram_tensor("wqT", [D, D], F32, kind="ExternalInput").ap()
    wkT_d = nc.dram_tensor("wkT", [D, 512], F32, kind="ExternalInput").ap()
    wvT_d = nc.dram_tensor("wvT", [D, 512], F32, kind="ExternalInput").ap()
    woT_d = nc.dram_tensor("woT", [D, D], F32, kind="ExternalInput").ap()
    wguT_d = nc.dram_tensor("wguT", [D, 2 * FF], F32, kind="ExternalInput").ap()
    wpT_d = nc.dram_tensor("wpT", [FF, D], F32, kind="ExternalInput").ap()
    sqd_d = nc.dram_tensor("sqd", [128, ND * 128], F32, kind="ExternalInput").ap()
    idb_d = nc.dram_tensor("idb", [128, 128], BF16, kind="ExternalInput").ap()
    idf_d = nc.dram_tensor("idf", [128, 128], F32, kind="ExternalInput").ap()
    bones_d = nc.dram_tensor("bonesc", [128, 2], BF16, kind="ExternalInput").ap()
    sel_d = nc.dram_tensor("selc", [2, 128], BF16, kind="ExternalInput").ap()
    y_d = nc.dram_tensor("y", [S, D], F32, kind="ExternalOutput").ap()

    q_sp = nc.dram_tensor("q_sp", [H, 128, S], BF16, kind=skind).ap()
    kT_sp = nc.dram_tensor("kT_sp", [G, 128, S], BF16, kind=skind).ap()
    v_sp = nc.dram_tensor("v_sp", [NT, 128, 512], BF16, kind=skind).ap()
    oT_sp = nc.dram_tensor("oT_sp", [H, 128, S], BF16, kind=skind).ap()
    l2t_sp = nc.dram_tensor("l2t_sp", [16, 128, S], BF16, kind=skind).ap()
    xm_sp = nc.dram_tensor("xm_sp", [S, D], F32, kind=skind).ap()
    wob_sp = nc.dram_tensor("wob_sp", [16, 128, D], BF16, kind=skind).ap()
    wgu_sp = nc.dram_tensor("wgu_sp", [32, 16, 128, 512], BF16, kind=skind).ap()
    wp_sp = nc.dram_tensor("wp_sp", [64, 128, D], BF16, kind=skind).ap()

    with ExitStack() as PS:
        id_b = PS.enter_context(nc.sbuf_tensor("id_b", [128, 128], BF16))
        id_f = PS.enter_context(nc.sbuf_tensor("id_f", [128, 128], F32))
        ones_col = PS.enter_context(nc.sbuf_tensor("ones_col", [128, 1], BF16))
        ones_row = PS.enter_context(nc.sbuf_tensor("ones_row", [1, 128], BF16))
        ones_full = PS.enter_context(nc.sbuf_tensor("ones_full", [128, 128], BF16))
        bones = PS.enter_context(nc.sbuf_tensor("bones", [128, 2], BF16))
        sel = PS.enter_context(nc.sbuf_tensor("sel", [2, 128], BF16))

        all_cts = []

        def mksem(name):
            c = CT(PS.enter_context(nc.semaphore(name)))
            all_cts.append(c)
            return c

        d_y = [DST(mksem(f"y_st{i}")) for i in range(2)]

        if "p0" in phases:
            P = Plan()
            with ExitStack() as SS:
                wo_tmp = [SS.enter_context(nc.sbuf_tensor(f"wotmp{i}", [128, D], BF16))
                          for i in range(2)]
                c_ld = mksem("p0a_ld")
                d_cast = [DST(mksem(f"p0a_cast{i}")) for i in range(2)]
                d_st = [DST(mksem(f"p0a_st{i}")) for i in range(2)]
                P.emit("sync", lambda e: e.dma_start(id_b[:], idb_d), (), (c_ld.tick(16),))
                P.emit("sync", lambda e: e.dma_start(id_f[:], idf_d), (), (c_ld.tick(16),))
                P.emit("vector", lambda e: e.memset(ones_col[:], 1.0))
                P.emit("vector", lambda e: e.memset(ones_row[:], 1.0))
                P.emit("vector", lambda e: e.memset(ones_full[:], 1.0))
                P.emit("sync", lambda e: e.dma_start(bones[:], bones_d), (),
                       (c_ld.tick(16),))
                P.emit("sync", lambda e: e.dma_start(sel[:], sel_d), (),
                       (c_ld.tick(16),))
                for of in range(16):
                    waits = [d_st[of % 2].ev()] if of >= 2 else []
                    d_cast[of % 2].emit(
                        P, "gpsimd",
                        lambda e, of=of: e.dma_start(
                            wo_tmp[of % 2][:], woT_d[128 * of:128 * (of + 1), :]),
                        waits)
                    d_st[of % 2].emit(
                        P, "scalar",
                        lambda e, of=of: e.dma_start(wob_sp[of], wo_tmp[of % 2][:]),
                        [d_cast[of % 2].ev()])
                P.emit("sync", lambda e: e.nop(),
                       [d_st[0].ev(), d_st[1].ev(), c_ld.ev()])
                P.check(all_cts); P.run_block(nc, "p0a")

        def make_p0b(P, SS):
            slab = [SS.enter_context(nc.sbuf_tensor(f"slab{i}", [128, FF], BF16))
                    for i in range(2)]
            aw_sb = [SS.enter_context(nc.sbuf_tensor(f"aw{i}", [128, 512], BF16))
                     for i in range(2)]
            sg_sb = [SS.enter_context(nc.sbuf_tensor(f"sg{i}", [128, 512], BF16))
                     for i in range(2)]
            sc_sb = [SS.enter_context(nc.sbuf_tensor(f"sc{i}", [2, 512], BF16))
                     for i in range(2)]
            msk_sb = [SS.enter_context(nc.sbuf_tensor(f"msk{i}", [128, 512], BF16))
                      for i in range(2)]
            t1_sb = [SS.enter_context(nc.sbuf_tensor(f"t1_{i}", [128, 512], BF16))
                     for i in range(2)]
            wh_sb = [SS.enter_context(nc.sbuf_tensor(f"wh{i}", [128, 512], BF16))
                     for i in range(2)]
            ps_q = [SS.enter_context(nc.psum_tensor(f"ps_q{i}", [128, 512], F32))
                    for i in range(2)]
            d_ld = [DST(mksem(f"q_ld{i}")) for i in range(2)]
            c_ab = mksem("q_ab")
            c_sum = mksem("q_sum")
            c_scl = mksem("q_scl")
            c_bc = mksem("q_bc")
            c_msk = mksem("q_msk")
            c_t1 = mksem("q_t1")
            c_wh = mksem("q_wh")
            d_sp = [DST(mksem(f"q_sp{i}")) for i in range(2)]

            jobs = []
            for dc in range(16):
                for hs in range(2):
                    src = wguT_d[128 * dc:128 * (dc + 1), FF * hs:FF * (hs + 1)]
                    dsts = [wgu_sp[16 * hs + w, dc] for w in range(16)]
                    jobs.append((src, FF, dsts))
            for c in range(64):
                src = wpT_d[128 * c:128 * (c + 1), :]
                dsts = [wp_sp[c][:, 512 * w:512 * (w + 1)] for w in range(4)]
                jobs.append((src, D, dsts))
            cum = [0]
            for _, _, dsts in jobs:
                cum.append(cum[-1] + len(dsts))

            ems = []
            nwin_c = [0]

            for ji, (src, width, dsts) in enumerate(jobs):
                def em_load(ji=ji, src=src, width=width):
                    waits = []
                    if ji >= 2:
                        waits.append(c_msk.ev(cum[ji - 1]))
                        waits.append(c_t1.ev(cum[ji - 1]))
                    d_ld[ji % 2].emit(P, "gpsimd",
                                      lambda e, s_=src, w_=width, ji=ji:
                                      e.dma_start(slab[ji % 2][:, 0:w_], s_),
                                      waits)
                ems.append(em_load)
                for w, dst in enumerate(dsts):
                    def em_win(ji=ji, w=w, dst=dst):
                        nwin_c[0] += 1
                        nwin = nwin_c[0]
                        p = nwin % 2
                        waits = [d_ld[ji % 2].ev()]
                        if nwin > 2:
                            waits.append(c_msk.ev(nwin - 2))
                        P.emit("scalar", lambda e, ji=ji, w=w, p=p:
                               e.activation(aw_sb[p][:],
                                            slab[ji % 2][:, 512 * w:512 * (w + 1)],
                                            AF.Abs),
                               waits, (c_ab.tick(),))
                        waits = []
                        if nwin > 2:
                            waits.append(c_t1.ev(nwin - 2))
                        P.emit("scalar", lambda e, ji=ji, w=w, p=p:
                               e.activation(sg_sb[p][:],
                                            slab[ji % 2][:, 512 * w:512 * (w + 1)],
                                            AF.Sign),
                               waits, (c_ab.tick(),))
                        waits = [c_ab.ev(2 * nwin - 1)]
                        if nwin > 2:
                            waits.append(c_wh.ev(nwin - 2))
                        P.emit("tensor", lambda e, p=p:
                               e.matmul(ps_q[p][0:2, :], bones[:], aw_sb[p][:],
                                        start=True, stop=True,
                                        skip_group_check=True),
                               waits, (c_sum.tick(),))
                        waits = [c_sum.ev(nwin)]
                        if nwin > 2:
                            waits.append(c_bc.ev(nwin - 2))
                        P.emit("vector", lambda e, p=p:
                               e.tensor_scalar(sc_sb[p][:], ps_q[p][0:2, :],
                                               1.0 / 64.0, 1e-8, ALU.mult, ALU.max),
                               waits, (c_scl.tick(),))
                        P.emit("tensor", lambda e, p=p:
                               e.matmul(ps_q[p][:], sel[:], sc_sb[p][:],
                                        start=True, stop=True,
                                        skip_group_check=True),
                               [c_scl.ev(nwin)], (c_bc.tick(),))
                        P.emit("vector", lambda e, p=p:
                               e.scalar_tensor_tensor(msk_sb[p][:], ps_q[p][:],
                                                      QTHR, aw_sb[p][:],
                                                      ALU.mult, ALU.is_lt),
                               [c_bc.ev(nwin)], (c_msk.tick(),))
                        P.emit("vector", lambda e, p=p:
                               e.tensor_tensor(t1_sb[p][:], msk_sb[p][:],
                                               sg_sb[p][:], ALU.mult),
                               [c_msk.ev(nwin)], (c_t1.tick(),))
                        waits = [c_t1.ev(nwin)]
                        if nwin > 2:
                            waits.append(d_sp[nwin % 2].ev())
                        P.emit("vector", lambda e, p=p, nw=nwin:
                               e.tensor_tensor(wh_sb[nw % 2][:], t1_sb[p][:],
                                               ps_q[p][:], ALU.mult),
                               waits, (c_wh.tick(),))
                        d_sp[nwin % 2].emit(P, "scalar",
                                            lambda e, dst=dst, nw=nwin:
                                            e.dma_start(dst, wh_sb[nw % 2][:]),
                                            [c_wh.ev(nwin)])
                    ems.append(em_win)
            return ems, d_sp


        if "a" in phases:
            P = Plan()
            with ExitStack() as SS:
                wq_sb = SS.enter_context(nc.sbuf_tensor("wq_sb", [128, 16, D], BF16))
                wk_sb = SS.enter_context(nc.sbuf_tensor("wk_sb", [128, 16, 512], BF16))
                wv_sb = SS.enter_context(nc.sbuf_tensor("wv_sb", [128, 16, 512], BF16))
                xb = [SS.enter_context(nc.sbuf_tensor(f"xa{i}", [128, D], F32))
                      for i in range(2)]
                lnb = [SS.enter_context(nc.sbuf_tensor(f"lnb{i}", [128, D], BF16))
                       for i in range(2)]
                lxT = SS.enter_context(nc.sbuf_tensor("lxT", [128, 16, 512], BF16))
                ev_sb = [SS.enter_context(nc.sbuf_tensor(f"aev{i}", [128, 512], BF16))
                         for i in range(4)]
                bns = SS.enter_context(nc.sbuf_tensor("bns", [128, 24], F32))
                mv = SS.enter_context(nc.sbuf_tensor("mv", [128, 2], F32))
                sdr = SS.enter_context(nc.sbuf_tensor("sdr", [128, 2], F32))
                ps_tr = [SS.enter_context(nc.psum_tensor(f"ps_tr{i}", [128, 128], BF16))
                         for i in range(2)]
                ps_mm = [SS.enter_context(nc.psum_tensor(f"ps_mm{i}", [128, 512], F32))
                         for i in range(2)]
                c_w = mksem("a_w")
                d_x = [DST(mksem(f"a_x{i}")) for i in range(2)]
                c_v = mksem("a_v")
                c_s = mksem("a_s")
                c_t = mksem("a_t")
                c_te = mksem("a_te")
                c_m = mksem("a_m")
                c_qe = mksem("a_qe")
                d_o = [DST(mksem(f"a_o{i}")) for i in range(4)]

                for wsb, wd in [(wq_sb, wqT_d), (wk_sb, wkT_d), (wv_sb, wvT_d)]:
                    for dc in range(16):
                        P.emit("gpsimd", lambda e, wsb=wsb, wd=wd, dc=dc:
                               e.dma_start(wsb[:, dc, :], wd[128 * dc:128 * (dc + 1), :]),
                               (), (c_w.tick(16),))
                VPT = 7
                for t in range(NT):
                    r, tt = t // 4, t % 4
                    waits = [c_v.ev(VPT * (t - 1))] if t >= 2 else []
                    d_x[t % 2].emit(P, "sync", lambda e, t=t:
                                    e.dma_start(xb[t % 2][:],
                                                x_d[128 * t:128 * (t + 1), :]),
                                    waits)
                    for i in range(4):
                        waits = [d_x[t % 2].ev()] if i == 0 else []
                        P.emit("vector", lambda e, t=t, i=i:
                               e.bn_stats(bns[:, 6 * i:6 * (i + 1)],
                                          xb[t % 2][:, 512 * i:512 * (i + 1)]),
                               waits, (c_v.tick(),))
                    P.emit("vector", lambda e: e.bn_aggr(mv[:], bns[:]),
                           [c_v.ev()], (c_v.tick(),))
                    P.emit("scalar", lambda e:
                           e.activation(sdr[:, 0:1], mv[:, 1:2], AF.Sqrt, bias=LN_EPS),
                           [c_v.ev()], (c_s.tick(),))
                    P.emit("vector", lambda e: e.reciprocal(sdr[:, 1:2], sdr[:, 0:1]),
                           [c_s.ev()], (c_v.tick(),))
                    waits = [c_v.ev()]
                    if t >= 2:
                        waits.append(c_t.ev(16 * (t - 1)))
                    P.emit("vector", lambda e, t=t:
                           e.tensor_scalar(lnb[t % 2][:], xb[t % 2][:],
                                           mv[:, 0:1], sdr[:, 1:2],
                                           ALU.subtract, ALU.mult),
                           waits, (c_v.tick(),))
                    for dc in range(16):
                        idx = 16 * t + dc + 1
                        waits = [c_v.ev(VPT * (t + 1))]
                        if idx > 2:
                            waits.append(c_te.ev(idx - 2))
                        P.emit("tensor", lambda e, t=t, dc=dc:
                               e.transpose(ps_tr[dc % 2][:],
                                           lnb[t % 2][:, 128 * dc:128 * (dc + 1)],
                                           id_b[:]),
                               waits, (c_t.tick(),))
                        ewaits = [c_t.ev(idx), c_m.ev(24 * r)]
                        P.emit("scalar", lambda e, tt=tt, dc=dc:
                               e.copy(lxT[:, dc, 128 * tt:128 * (tt + 1)],
                                      ps_tr[dc % 2][:]),
                               ewaits, (c_te.tick(),))
                    if tt == 3:
                        for i in range(24):
                            nev = 24 * r + i
                            p = ps_mm[nev % 2]
                            waits = [c_w.ev(16 * 48), c_te.ev(64 * (r + 1))]
                            if nev >= 2:
                                waits.append(c_qe.ev(nev - 1))

                            def mk(e, i=i, p=p):
                                if i < 16:
                                    for dc in range(16):
                                        inst = e.matmul(
                                            p[:], wq_sb[:, dc, 128 * i:128 * (i + 1)],
                                            lxT[:, dc, :],
                                            start=(dc == 0), stop=(dc == 15),
                                            skip_group_check=True)
                                elif i < 20:
                                    kf = i - 16
                                    for dc in range(16):
                                        inst = e.matmul(
                                            p[:], wk_sb[:, dc, 128 * kf:128 * (kf + 1)],
                                            lxT[:, dc, :],
                                            start=(dc == 0), stop=(dc == 15),
                                            skip_group_check=True)
                                else:
                                    tv = i - 20
                                    for dc in range(16):
                                        inst = e.matmul(
                                            p[:], lxT[:, dc, 128 * tv:128 * (tv + 1)],
                                            wv_sb[:, dc, :],
                                            start=(dc == 0), stop=(dc == 15),
                                            skip_group_check=True)
                                return inst
                            P.emit("tensor", mk, waits, (c_m.tick(),))
                            buf = ev_sb[nev % 4]
                            waits = [c_m.ev(nev + 1)]
                            if nev >= 4:
                                waits.append(d_o[nev % 4].ev())
                            if i < 16:
                                P.emit("scalar", lambda e, buf=buf, p=p:
                                       e.activation(buf[:], p[:], AF.Copy,
                                                    bias=0.0, scale=ATT_SCALE),
                                       waits, (c_qe.tick(),))
                            else:
                                P.emit("scalar", lambda e, buf=buf, p=p:
                                       e.copy(buf[:], p[:]),
                                       waits, (c_qe.tick(),))
                            if i < 16:
                                dst = q_sp[i][:, 512 * r:512 * (r + 1)]
                            elif i < 20:
                                dst = kT_sp[i - 16][:, 512 * r:512 * (r + 1)]
                            else:
                                dst = v_sp[4 * r + (i - 20)]
                            d_o[nev % 4].emit(P, "scalar",
                                              lambda e, dst=dst, buf=buf:
                                              e.dma_start(dst, buf[:]),
                                              [c_qe.ev(nev + 1)])
                P.emit("sync", lambda e: e.nop(), [d.ev() for d in d_o])
                P.check(all_cts); P.run_block(nc, "a")

        if "b1" in phases:
            P = Plan()
            with ExitStack() as SS:
                sqd_sb = SS.enter_context(nc.sbuf_tensor("sqd_sb", [128, ND * 128], F32))
                ebm = SS.enter_context(nc.sbuf_tensor("ebm", [128, ND * 128], F32))
                ebt = [SS.enter_context(nc.sbuf_tensor(f"ebt{i}", [128, ND * 128], BF16))
                       for i in range(4)]
                kbuf = SS.enter_context(nc.sbuf_tensor("kbuf", [128, S], BF16))
                vbuf = SS.enter_context(nc.sbuf_tensor("vbuf", [128, S], BF16))
                qbuf = [[SS.enter_context(nc.sbuf_tensor(f"qbuf{i}_{p}", [128, 512], BF16))
                         for p in range(2)] for i in range(4)]
                ex_sb = [SS.enter_context(nc.sbuf_tensor(f"ex{i}", [128, 512], BF16))
                         for i in range(2)]
                p_sb = [SS.enter_context(nc.sbuf_tensor(f"p_sb{i}", [128, 512], BF16))
                        for i in range(4)]
                l_sb = SS.enter_context(nc.sbuf_tensor("l_sb", [1, 1024], F32))
                lr_sb = SS.enter_context(nc.sbuf_tensor("lr_sb", [1, 1024], F32))
                lb_sb = SS.enter_context(nc.sbuf_tensor("lb_sb", [1, 1024], BF16))
                rb_sb = SS.enter_context(nc.sbuf_tensor("rb_sb", [128, 512], F32))
                o_sb = [SS.enter_context(nc.sbuf_tensor(f"o_sb{i}", [128, 512], BF16))
                        for i in range(2)]
                ps_o = [SS.enter_context(nc.psum_tensor(f"ps_o{i}", [128, 512], F32))
                        for i in range(2)]
                ps_sc = [SS.enter_context(nc.psum_tensor(f"ps_sc{i}", [128, 512], F32))
                         for i in range(4)]
                ps_l = [SS.enter_context(nc.psum_tensor(f"ps_l{i}", [1, 512], F32))
                        for i in range(2)]
                d_sqd = DST(mksem("b1_sqd"))
                d_k = DST(mksem("b1_k"))
                d_v = DST(mksem("b1_v"))
                d_q = [DST(mksem(f"b1_q{i}")) for i in range(4)]
                c_eb = mksem("b1_eb")
                c_ebs = mksem("b1_ebs")
                c_qk = mksem("b1_qk")
                c_ex = mksem("b1_ex")
                c_pm = mksem("b1_pm")
                c_lm = mksem("b1_lm")
                c_pv = mksem("b1_pv")
                c_tl = mksem("b1_tl")
                c_bc = mksem("b1_bc")
                d_os = [DST(mksem(f"b1_os{i}")) for i in range(2)]

                d_sqd.emit(P, "sync", lambda e: e.dma_start(sqd_sb[:], sqd_d))
                scfree = [None] * 4
                exfree = [None] * 2
                pfree = [None] * 4
                ofree = [None] * 2
                lfree = [None] * 2
                qk_end = {}
                import os as _os
                _NG = int(_os.environ.get("B1_G", G))
                _NRR = int(_os.environ.get("B1_R", NR))
                for g in range(_NG):
                    qk_g0 = c_qk.n
                    pv_g0 = c_pv.n
                    pm_g0 = c_pm.n
                    for j in range(4):
                        h = 4 * g + j
                        waits = [d_sqd.ev()]
                        if h >= 1:
                            waits.append(c_ebs.ev(h))
                        P.emit("vector", lambda e, h=h:
                               e.tensor_scalar_mul(ebm[:], sqd_sb[:], -SLOPES[h]),
                               waits, (c_eb.tick(),))
                        waits = [c_eb.ev(h + 1)]
                        if g >= 1:
                            waits.append(c_pm.ev(pm_g0))
                        P.emit("scalar", lambda e, j=j:
                               e.activation(ebt[j][:], ebm[:], AF.Exp),
                               waits, (c_ebs.tick(),))
                    for r in range(_NRR):
                        nu = 4 * r + 4
                        waits = [c_qk.ev(qk_g0)] if g >= 1 else []
                        d_k.emit(P, "sync", lambda e, g=g, r=r:
                                 e.dma_start(kbuf[:, 512 * r:512 * (r + 1)],
                                             kT_sp[g][:, 512 * r:512 * (r + 1)]),
                                 waits)
                        for uu in range(4):
                            u = 4 * r + uu
                            waits = [c_pv.ev(pv_g0)] if g >= 1 else []
                            d_v.emit(P, "sync", lambda e, g=g, u=u:
                                     e.dma_start(vbuf[:, 128 * u:128 * (u + 1)],
                                                 v_sp[u][:, 128 * g:128 * (g + 1)]),
                                     waits)
                        for j in range(4):
                            h = 4 * g + j
                            waits = []
                            if (g, r - 2) in qk_end:
                                waits.append(c_qk.ev(qk_end[(g, r - 2)]))
                            elif g >= 1 and r < 2:
                                waits.append(c_qk.ev(qk_g0))
                            d_q[j].emit(P, "sync", lambda e, h=h, r=r, j=j:
                                        e.dma_start(qbuf[j][r % 2][:],
                                                    q_sp[h][:, 512 * r:512 * (r + 1)]),
                                        waits)
                        ld_nows = (d_k.ev(), d_v.ev(), d_q[0].ev(), d_q[1].ev(),
                                   d_q[2].ev(), d_q[3].ev())
                        for pr in range(2):
                            pvl = []
                            for u in range(nu):
                                for j in range(2):
                                    rp = 2 * pr + j
                                    b = 2 * pr + j
                                    waits = []
                                    if pr == 0 and u == 0 and j == 0:
                                        waits += list(ld_nows)
                                    if scfree[b] is not None:
                                        waits.append(scfree[b])
                                    P.emit("tensor", lambda e, u=u, rp=rp, b=b, r=r:
                                           e.matmul(ps_sc[b][:],
                                                    kbuf[:, 128 * u:128 * (u + 1)],
                                                    qbuf[rp][r % 2][:],
                                                    start=True, stop=True,
                                                    skip_group_check=True),
                                           waits, (c_qk.tick(),))
                                    qk_this = c_qk.n
                                    waits = [c_qk.ev(qk_this)]
                                    if exfree[j] is not None:
                                        waits.append(exfree[j])
                                    P.emit("scalar", lambda e, j=j, b=b:
                                           e.activation(ex_sb[j][:], ps_sc[b][:],
                                                        AF.Exp, bias=CSHIFT),
                                           waits, (c_ex.tick(),))
                                    scfree[b] = c_ex.ev()
                                    pi = 2 * (u % 2) + j
                                    col = 128 * (4 * r - u + 3)
                                    waits = [c_ex.ev(), c_ebs.ev(4 * g + rp + 1)]
                                    if pfree[pi] is not None:
                                        waits.append(pfree[pi])
                                    P.emit("vector",
                                           lambda e, j=j, pi=pi, col=col, rp=rp:
                                           e.tensor_tensor(p_sb[pi][:], ex_sb[j][:],
                                                           ebt[rp][:, col:col + 512],
                                                           ALU.mult),
                                           waits, (c_pm.tick(),))
                                    exfree[j] = c_pm.ev()
                                    pm_this = c_pm.ev()
                                    def mk_pvl(u=u, j=j, pi=pi, nu=nu, pm_this=pm_this):
                                        waits = [pm_this]
                                        if u == 0 and ofree[j] is not None:
                                            waits.append(ofree[j])
                                        inc = [c_pv.tick()] if u == nu - 1 else ()
                                        P.emit("tensor",
                                               lambda e, u=u, j=j, pi=pi, nu=nu:
                                               e.matmul(ps_o[j][:],
                                                        vbuf[:, 128 * u:128 * (u + 1)],
                                                        p_sb[pi][:],
                                                        start=(u == 0),
                                                        stop=(u == nu - 1),
                                                        skip_group_check=True),
                                               waits, inc)
                                        waits = []
                                        if u == 0 and lfree[j] is not None:
                                            waits.append(lfree[j])
                                        P.emit("tensor",
                                               lambda e, u=u, j=j, pi=pi, nu=nu:
                                               e.matmul(ps_l[j][:], ones_col[:],
                                                        p_sb[pi][:],
                                                        start=(u == 0),
                                                        stop=(u == nu - 1),
                                                        skip_group_check=True),
                                               waits, (c_lm.tick(),))
                                        pfree[pi] = c_lm.ev()
                                    pvl.append(mk_pvl)
                                if u >= 1:
                                    pvl.pop(0)()
                                    pvl.pop(0)()
                            while pvl:
                                pvl.pop(0)()
                            for j in range(2):
                                h = 4 * g + 2 * pr + j
                                b = 2 * pr + j
                                P.emit("vector", lambda e, j=j:
                                       e.tensor_copy(l_sb[:, 512 * j:512 * (j + 1)],
                                                     ps_l[j][:]),
                                       [c_lm.ev()], (c_tl.tick(),))
                                lfree[j] = c_tl.ev()
                                P.emit("vector", lambda e, j=j:
                                       e.reciprocal(lr_sb[:, 512 * j:512 * (j + 1)],
                                                    l_sb[:, 512 * j:512 * (j + 1)]),
                                       [c_tl.ev()], (c_tl.tick(),))
                                P.emit("vector", lambda e, j=j:
                                       e.tensor_copy(lb_sb[:, 512 * j:512 * (j + 1)],
                                                     lr_sb[:, 512 * j:512 * (j + 1)]),
                                       [c_tl.ev()], (c_tl.tick(),))
                                waits = [c_tl.ev(), scfree[b]]
                                P.emit("tensor", lambda e, j=j, b=b:
                                       e.matmul(ps_sc[b][:], ones_row[:],
                                                lb_sb[:, 512 * j:512 * (j + 1)],
                                                start=True, stop=True,
                                                skip_group_check=True),
                                       waits, (c_bc.tick(),))
                                P.emit("vector", lambda e, b=b:
                                       e.tensor_copy(rb_sb[:], ps_sc[b][:]),
                                       [c_bc.ev()], (c_tl.tick(),))
                                scfree[b] = c_tl.ev()
                                waits = [c_tl.ev(), c_pv.ev()]
                                if d_os[j].n > 0:
                                    waits.append(d_os[j].ev())
                                P.emit("vector", lambda e, j=j:
                                       e.tensor_tensor(o_sb[j][:], ps_o[j][:],
                                                       rb_sb[:], ALU.mult),
                                       waits, (c_tl.tick(),))
                                ofree[j] = c_tl.ev()
                                d_os[j].emit(P, "scalar",
                                             lambda e, h=h, r=r, j=j:
                                             e.dma_start(
                                                 oT_sp[h][:, 512 * r:512 * (r + 1)],
                                                 o_sb[j][:]),
                                             [c_tl.ev()])
                        qk_end[(g, r)] = c_qk.n
                P.emit("sync", lambda e: e.nop(), [d_os[0].ev(), d_os[1].ev()])
                P.check(all_cts); P.run_block(nc, "b1")


        if "b2" in phases:
            P = Plan()
            with ExitStack() as SS:
                if "p0" in phases:
                    p0b_ems, d_spq = make_p0b(P, SS)
                else:
                    p0b_ems, d_spq = [], None
                def pop_p0b(k=2):
                    for _ in range(k):
                        if p0b_ems:
                            p0b_ems.pop(0)()
                oT_buf = SS.enter_context(nc.sbuf_tensor("oT_buf", [128, 16, 512], BF16))
                wo_buf = [SS.enter_context(nc.sbuf_tensor(f"wo_buf{i}", [128, 512], BF16))
                          for i in range(4)]
                xb2 = [SS.enter_context(nc.sbuf_tensor(f"xb2_{i}", [128, D], F32))
                       for i in range(2)]
                ao4 = [SS.enter_context(nc.sbuf_tensor(f"ao{i}", [128, D], F32))
                       for i in range(4)]
                t2_sb = SS.enter_context(nc.sbuf_tensor("t2_sb", [128, D], F32))
                xm_sb = [SS.enter_context(nc.sbuf_tensor(f"xmb{i}", [128, D], F32))
                         for i in range(2)]
                ln2_sb = SS.enter_context(nc.sbuf_tensor("ln2_sb", [128, D], BF16))
                l2t_sb = [SS.enter_context(nc.sbuf_tensor(f"l2t{i}", [128, 128], BF16))
                          for i in range(2)]
                bns2 = SS.enter_context(nc.sbuf_tensor("bns2", [128, 24], F32))
                mv2 = SS.enter_context(nc.sbuf_tensor("mv2", [128, 2], F32))
                sdr2 = SS.enter_context(nc.sbuf_tensor("sdr2", [128, 2], F32))
                ps_b2 = [SS.enter_context(nc.psum_tensor(f"ps_b2_{i}", [128, 512], F32))
                         for i in range(4)]
                ps_t2 = [SS.enter_context(nc.psum_tensor(f"ps_t2_{i}", [128, 128], BF16))
                         for i in range(2)]
                c_lo = mksem("b2_lo")
                d_wo = [DST(mksem(f"b2_wo{i}")) for i in range(4)]
                d_x2 = [DST(mksem(f"b2_x{i}")) for i in range(2)]
                c_m = mksem("b2_m")
                c_e = mksem("b2_e")
                c_v = mksem("b2_v")
                c_s = mksem("b2_s")
                c_t = mksem("b2_t")
                c_te = mksem("b2_te")
                d_sxm = [DST(mksem(f"b2_sxm{i}")) for i in range(2)]
                d_sl2 = [DST(mksem(f"b2_sl2{i}")) for i in range(2)]

                add2_ev = {}
                for r in range(NR):
                    m_r0 = c_m.n
                    for of in range(16):
                        waits = [c_m.ev()] if (r >= 1 and of == 0) else []
                        P.emit("sync", lambda e, of=of, r=r:
                               e.dma_start(oT_buf[:, of, :],
                                           oT_sp[of][:, 512 * r:512 * (r + 1)]),
                               waits, (c_lo.tick(16),))
                    for q4 in range(4):
                        for of in range(16):
                            gi = 16 * q4 + of
                            waits = []
                            if gi >= 4:
                                waits.append(c_m.ev(m_r0 + 4 * (gi - 3)))
                            d_wo[gi % 4].emit(
                                P, "sync", lambda e, q4=q4, of=of, gi=gi:
                                e.dma_start(wo_buf[gi % 4][:],
                                            wob_sp[of][:, 512 * q4:512 * (q4 + 1)]),
                                waits)
                            for tt in range(4):
                                waits = [c_lo.ev(16 * 16 * (r + 1)),
                                         d_wo[gi % 4].ev()]
                                if of == 0:
                                    prev_evac = 16 * r + 4 * q4 - 4 + tt + 1
                                    if prev_evac > 0:
                                        waits.append(c_e.ev(prev_evac))
                                P.emit("tensor", lambda e, q4=q4, of=of, tt=tt, gi=gi:
                                       e.matmul(ps_b2[tt][:],
                                                oT_buf[:, of, 128 * tt:128 * (tt + 1)],
                                                wo_buf[gi % 4][:],
                                                start=(of == 0), stop=(of == 15),
                                                skip_group_check=True),
                                       waits, (c_m.tick(),))
                            pop_p0b(2)
                        for tt in range(4):
                            P.emit("vector", lambda e, q4=q4, tt=tt:
                                   e.tensor_copy(ao4[tt][:, 512 * q4:512 * (q4 + 1)],
                                                 ps_b2[tt][:]),
                                   [c_m.ev()], (c_e.tick(),))
                    for tt in range(4):
                        t = 4 * r + tt
                        lwaits = []
                        if t >= 2:
                            lwaits.append((c_v.sem, add2_ev[t - 2]))
                        d_x2[t % 2].emit(P, "sync", lambda e, t=t:
                                         e.dma_start(xb2[t % 2][:],
                                                     x_d[128 * t:128 * (t + 1), :]),
                                         lwaits)
                        for i in range(4):
                            waits = [c_e.ev()]
                            if i > 0:
                                waits = [c_v.ev()]
                            P.emit("vector", lambda e, tt=tt, i=i:
                                   e.bn_stats(bns2[:, 6 * i:6 * (i + 1)],
                                              ao4[tt][:, 512 * i:512 * (i + 1)]),
                                   waits, (c_v.tick(),))
                        P.emit("vector", lambda e: e.bn_aggr(mv2[:], bns2[:]),
                               [c_v.ev()], (c_v.tick(),))
                        P.emit("scalar", lambda e:
                               e.activation(sdr2[:, 0:1], mv2[:, 1:2], AF.Sqrt,
                                            bias=LN_EPS),
                               [c_v.ev()], (c_s.tick(),))
                        P.emit("vector", lambda e:
                               e.reciprocal(sdr2[:, 1:2], sdr2[:, 0:1]),
                               [c_s.ev()], (c_v.tick(),))
                        P.emit("vector", lambda e, tt=tt:
                               e.tensor_scalar(t2_sb[:], ao4[tt][:],
                                               mv2[:, 0:1], sdr2[:, 1:2],
                                               ALU.subtract, ALU.mult),
                               [c_v.ev()], (c_v.tick(),))
                        waits = [c_v.ev(), d_x2[t % 2].ev()]
                        if t >= 2:
                            waits.append(d_sxm[t % 2].ev())
                        P.emit("vector", lambda e, t=t, tt=tt:
                               e.tensor_tensor(xm_sb[t % 2][:], t2_sb[:],
                                               xb2[t % 2][:], ALU.add),
                               waits, (c_v.tick(),))
                        add2_ev[t] = c_v.n
                        d_sxm[t % 2].emit(P, "scalar", lambda e, t=t:
                                          e.dma_start(xm_sp[128 * t:128 * (t + 1), :],
                                                      xm_sb[t % 2][:]),
                                          [c_v.ev()])
                        for i in range(4):
                            P.emit("vector", lambda e, t=t, i=i:
                                   e.bn_stats(bns2[:, 6 * i:6 * (i + 1)],
                                              xm_sb[t % 2][:, 512 * i:512 * (i + 1)]),
                                   [c_v.ev()] if i == 0 else (), (c_v.tick(),))
                        P.emit("vector", lambda e: e.bn_aggr(mv2[:], bns2[:]),
                               [c_v.ev()], (c_v.tick(),))
                        P.emit("scalar", lambda e:
                               e.activation(sdr2[:, 0:1], mv2[:, 1:2], AF.Sqrt,
                                            bias=LN_EPS),
                               [c_v.ev()], (c_s.tick(),))
                        P.emit("vector", lambda e:
                               e.reciprocal(sdr2[:, 1:2], sdr2[:, 0:1]),
                               [c_s.ev()], (c_v.tick(),))
                        P.emit("vector", lambda e, t=t:
                               e.tensor_scalar(ln2_sb[:], xm_sb[t % 2][:],
                                               mv2[:, 0:1], sdr2[:, 1:2],
                                               ALU.subtract, ALU.mult),
                               [c_v.ev()], (c_v.tick(),))
                        for dc in range(16):
                            idx = c_t.n + 1
                            waits = [c_v.ev()]
                            if idx > 2:
                                waits.append(c_te.ev(idx - 2))
                            P.emit("tensor", lambda e, dc=dc:
                                   e.transpose(ps_t2[dc % 2][:],
                                               ln2_sb[:, 128 * dc:128 * (dc + 1)],
                                               id_b[:]),
                                   waits, (c_t.tick(),))
                            waits = [c_t.ev()]
                            if d_sl2[dc % 2].n > 0:
                                waits.append(d_sl2[dc % 2].ev())
                            P.emit("scalar", lambda e, dc=dc:
                                   e.copy(l2t_sb[dc % 2][:], ps_t2[dc % 2][:]),
                                   waits, (c_te.tick(),))
                            d_sl2[dc % 2].emit(
                                P, "scalar", lambda e, dc=dc, t=t:
                                e.dma_start(l2t_sp[dc][:, 128 * t:128 * (t + 1)],
                                            l2t_sb[dc % 2][:]),
                                [c_te.ev()])
                while p0b_ems:
                    p0b_ems.pop(0)()
                drains = [d_sxm[0].ev(), d_sxm[1].ev(), d_sl2[0].ev(), d_sl2[1].ev()]
                if d_spq is not None:
                    drains += [d_spq[0].ev(), d_spq[1].ev()]
                P.emit("sync", lambda e: e.nop(), drains)
                P.check(all_cts); P.run_block(nc, "b2")

        if "c" in phases:
            P = Plan()
            with ExitStack() as SS:
                l2T_buf = SS.enter_context(nc.sbuf_tensor("l2T_buf", [128, 16, 512], BF16))
                wgu_buf = [SS.enter_context(nc.sbuf_tensor(f"wgub{i}", [128, 16, 512], BF16))
                           for i in range(3)]
                wp_buf = [SS.enter_context(nc.sbuf_tensor(f"wpb{i}", [128, 512], BF16))
                          for i in range(4)]
                h_st = SS.enter_context(nc.sbuf_tensor("h_st", [128, 64, 512], BF16))
                silu_sb = SS.enter_context(nc.sbuf_tensor("silu_sb", [128, 512], F32))
                hsq_sb = SS.enter_context(nc.sbuf_tensor("hsq_sb", [128, 512], BF16))
                ss_sb = SS.enter_context(nc.sbuf_tensor("ss_sb", [128, 512], F32))
                rs_sb = SS.enter_context(nc.sbuf_tensor("rs_sb", [128, 8], F32))
                ffn4 = [SS.enter_context(nc.sbuf_tensor(f"ffn{i}", [128, D], F32))
                        for i in range(4)]
                xmC = [SS.enter_context(nc.sbuf_tensor(f"xmc{i}", [128, D], F32))
                       for i in range(2)]
                t2C = SS.enter_context(nc.sbuf_tensor("t2C", [128, D], F32))
                bnsC = SS.enter_context(nc.sbuf_tensor("bnsC", [128, 24], F32))
                mvC = SS.enter_context(nc.sbuf_tensor("mvC", [128, 2], F32))
                sdrC = SS.enter_context(nc.sbuf_tensor("sdrC", [128, 2], F32))
                ps_W = [SS.enter_context(nc.psum_tensor(f"ps_W{i}", [128, 512], F32))
                        for i in range(4)]
                ps_ss = SS.enter_context(nc.psum_tensor("ps_ss", [128, 512], F32))
                ps_tc = SS.enter_context(nc.psum_tensor("ps_tc", [128, 128], F32))
                c_l2 = mksem("c_l2")
                c_wg = mksem("c_wg")
                d_wp = [DST(mksem(f"c_wp{i}")) for i in range(4)]
                d_xm = [DST(mksem(f"c_xm{i}")) for i in range(2)]
                c_g = mksem("c_g")
                c_sl = mksem("c_sl")
                c_h = mksem("c_h")
                c_hq = mksem("c_hq")
                c_ssm = mksem("c_ssm")
                c_sse = mksem("c_sse")
                c_tc = mksem("c_tc")
                c_rs = mksem("c_rs")
                c_rsv = mksem("c_rsv")
                c_2m = mksem("c_2m")
                c_2e = mksem("c_2e")
                c_v = mksem("c_v")
                c_s = mksem("c_s")

                add_ev = {}
                yspill = {}
                wld_idx = [0]
                wld_base = [0]
                wld_done = {}
                wld_buf = {}
                import os as _os
                _NRC = int(_os.environ.get("C_R", NR))
                _CST = _os.environ.get("C_STAGE", "full")
                for r in range(_NRC):
                    g_r0 = c_g.n
                    m2_r0 = c_2m.n
                    for dc in range(16):
                        waits = []
                        if r >= 1 and dc == 0:
                            waits.append(c_g.ev())
                        P.emit("sync", lambda e, dc=dc, r=r:
                               e.dma_start(l2T_buf[:, dc, :],
                                           l2t_sp[dc][:, 512 * r:512 * (r + 1)]),
                               waits, (c_l2.tick(16),))
                    _NHCQ = int(_os.environ.get("C_HCQ", 16))
                    wld_base[0] = wld_idx[0]
                    for hcq in range(_NHCQ):
                        for part in range(2):
                            quad = hcq if part == 0 else hcq + 16
                            ldi = wld_idx[0]
                            wld_idx[0] += 1
                            bi = ldi % 3
                            waits = []
                            if ldi >= 3 and (ldi - 3) in wld_done:
                                waits.append(c_g.ev(wld_done[ldi - 3]))
                            for dcq in range(16):
                                P.emit("sync", lambda e, quad=quad, bi=bi, dcq=dcq:
                                       e.dma_start(wgu_buf[bi][:, dcq, :],
                                                   wgu_sp[quad, dcq]),
                                       waits if dcq == 0 else (),
                                       (c_wg.tick(16),))
                            wld_buf[(hcq, part)] = bi
                        for sub in range(4):
                            hc = 4 * hcq + sub
                            for part in range(2):
                                pw = ps_W[2 * part + (sub % 2)]
                                waits = [c_wg.ev(), c_l2.ev(16 * 16 * (r + 1))]
                                need = 64 * r + hc - 1
                                if need > 0:
                                    waits.append((c_sl if part == 0 else c_h).ev(need))
                                if hcq == 0 and sub == 0 and r >= 1:
                                    waits.append(c_2e.ev(16 * r))

                                bi = wld_buf[(hcq, part)]

                                def mkg(e, bi=bi, sub=sub, pw=pw):
                                    for dc in range(16):
                                        inst = e.matmul(
                                            pw[:], wgu_buf[bi][:, dc, 128 * sub:128 * (sub + 1)],
                                            l2T_buf[:, dc, :],
                                            start=(dc == 0), stop=(dc == 15),
                                            skip_group_check=True)
                                    return inst
                                P.emit("tensor", mkg, waits, (c_g.tick(),))
                                if sub == 3:
                                    ldi_done = 2 * hcq + part + wld_base[0]
                                    wld_done[ldi_done] = c_g.n
                        for sub in range(4):
                            hc = 4 * hcq + sub
                            pg = ps_W[0 + (sub % 2)]
                            pu = ps_W[2 + (sub % 2)]
                            gate_ev = g_r0 + 8 * hcq + 2 * sub + 1
                            up_ev = g_r0 + 8 * hcq + 2 * sub + 2
                            waits = [c_g.ev(gate_ev)]
                            if c_h.n > 0:
                                waits.append(c_h.ev())
                            P.emit("scalar", lambda e, pg=pg:
                                   e.activation(silu_sb[:], pg[:], AF.Silu),
                                   waits, (c_sl.tick(),))
                            hwaits = [c_sl.ev(), c_g.ev(up_ev)]
                            if r >= 1:
                                hwaits.append(c_2m.ev(1024 * r))
                            P.emit("vector", lambda e, pu=pu, hc=hc:
                                   e.tensor_tensor(h_st[:, hc, :], silu_sb[:],
                                                   pu[:], ALU.mult),
                                   hwaits, (c_h.tick(),))
                            waits = [c_h.ev()]
                            if c_ssm.n > 0:
                                waits.append(c_ssm.ev())
                            P.emit("scalar", lambda e, hc=hc:
                                   e.activation(hsq_sb[:], h_st[:, hc, :], AF.Square),
                                   waits, (c_hq.tick(),))
                            if _CST != "c0":
                                sswaits = [c_hq.ev()]
                                if hc == 0 and r >= 1:
                                    sswaits.append(c_sse.ev(r))
                                P.emit("tensor", lambda e, hc=hc:
                                       e.matmul(ps_ss[:], ones_full[:], hsq_sb[:],
                                                start=(hc == 0), stop=(hc == 4 * _NHCQ - 1),
                                                skip_group_check=True),
                                       sswaits, (c_ssm.tick(),))
                    if _CST in ("c0",):
                        continue
                    sswaits = [c_ssm.ev()]
                    if r >= 1:
                        sswaits.append(c_tc.ev(4 * r))
                    P.emit("vector", lambda e:
                           e.tensor_copy(ss_sb[:], ps_ss[:]),
                           sswaits, (c_sse.tick(),))
                    for tt in range(4):
                        waits = [c_sse.ev()]
                        if tt > 0:
                            waits.append(c_rs.ev())
                        P.emit("tensor", lambda e, tt=tt:
                               e.transpose(ps_tc[:], ss_sb[:, 128 * tt:128 * (tt + 1)],
                                           id_f[:]),
                               waits, (c_tc.tick(),))
                        rwaits = [c_tc.ev()]
                        if r >= 1:
                            rwaits.append(c_2e.ev(16 * r))
                        P.emit("scalar", lambda e, tt=tt:
                               e.activation(rs_sb[:, 2 * tt:2 * tt + 1],
                                            ps_tc[:, 0:1], AF.Sqrt,
                                            bias=RMS_EPS, scale=f32(1.0 / FF)),
                               rwaits, (c_rs.tick(),))
                        P.emit("vector", lambda e, tt=tt:
                               e.reciprocal(rs_sb[:, 2 * tt + 1:2 * tt + 2],
                                            rs_sb[:, 2 * tt:2 * tt + 1]),
                               [c_rs.ev()], (c_rsv.tick(),))
                    for q4 in range(4 if _CST in ("c2", "full") else 0):
                        for c in range(64):
                            gi = 64 * q4 + c
                            waits = []
                            if gi >= 4:
                                waits.append(c_2m.ev(m2_r0 + 4 * (gi - 3)))
                            d_wp[gi % 4].emit(
                                P, "sync", lambda e, c=c, q4=q4, gi=gi:
                                e.dma_start(wp_buf[gi % 4][:],
                                            wp_sp[c][:, 512 * q4:512 * (q4 + 1)]),
                                waits)
                            for tt in range(4):
                                waits = [d_wp[gi % 4].ev()]
                                if c == 0:
                                    waits.append(c_h.ev(64 * (r + 1)))
                                    if q4 >= 1:
                                        waits.append(c_2e.ev(16 * r + 4 * (q4 - 1) + tt + 1))
                                P.emit("tensor", lambda e, c=c, tt=tt, gi=gi:
                                       e.matmul(ps_W[tt][:],
                                                h_st[:, c, 128 * tt:128 * (tt + 1)],
                                                wp_buf[gi % 4][:],
                                                start=(c == 0), stop=(c == 63),
                                                skip_group_check=True),
                                       waits, (c_2m.tick(),))
                        for tt in range(4):
                            waits = [c_2m.ev(), c_rsv.ev(4 * (r + 1))]
                            if q4 == 0 and tt in yspill:
                                waits.append(yspill[tt])
                            P.emit("vector", lambda e, q4=q4, tt=tt:
                                   e.tensor_scalar_mul(
                                       ffn4[tt][:, 512 * q4:512 * (q4 + 1)],
                                       ps_W[tt][:], rs_sb[:, 2 * tt + 1:2 * tt + 2]),
                                   waits, (c_2e.tick(),))
                    for tt in range(4 if _CST == "full" else 0):
                        t = 4 * r + tt
                        lwaits = []
                        gt = 4 * r + tt
                        if gt >= 2:
                            lwaits.append((c_v.sem, add_ev[gt - 2]))
                        d_xm[tt % 2].emit(P, "sync", lambda e, t=t, tt=tt:
                                          e.dma_start(xmC[tt % 2][:],
                                                      xm_sp[128 * t:128 * (t + 1), :]),
                                          lwaits)
                        for i in range(4):
                            P.emit("vector", lambda e, tt=tt, i=i:
                                   e.bn_stats(bnsC[:, 6 * i:6 * (i + 1)],
                                              ffn4[tt][:, 512 * i:512 * (i + 1)]),
                                   [c_2e.ev()] if i == 0 else (), (c_v.tick(),))
                        P.emit("vector", lambda e: e.bn_aggr(mvC[:], bnsC[:]),
                               [c_v.ev()], (c_v.tick(),))
                        P.emit("scalar", lambda e:
                               e.activation(sdrC[:, 0:1], mvC[:, 1:2], AF.Sqrt,
                                            bias=LN_EPS),
                               [c_v.ev()], (c_s.tick(),))
                        P.emit("vector", lambda e:
                               e.reciprocal(sdrC[:, 1:2], sdrC[:, 0:1]),
                               [c_s.ev()], (c_v.tick(),))
                        P.emit("vector", lambda e, tt=tt:
                               e.tensor_scalar(t2C[:], ffn4[tt][:],
                                               mvC[:, 0:1], sdrC[:, 1:2],
                                               ALU.subtract, ALU.mult),
                               [c_v.ev()], (c_v.tick(),))
                        waits = [c_v.ev(), d_xm[tt % 2].ev()]
                        P.emit("vector", lambda e, tt=tt:
                               e.tensor_tensor(ffn4[tt][:], t2C[:],
                                               xmC[tt % 2][:], ALU.add),
                               waits, (c_v.tick(),))
                        add_ev[gt] = c_v.n
                        d_y[tt % 2].emit(P, "scalar", lambda e, t=t, tt=tt:
                                         e.dma_start(y_d[128 * t:128 * (t + 1), :],
                                                     ffn4[tt][:]),
                                         [c_v.ev()])
                        yspill[tt] = d_y[tt % 2].ev()
                P.check(all_cts); P.run_block(nc, "c")

            Pf = Plan()
            if d_y[0].n > 0 or d_y[1].n > 0:
                Pf.emit("sync", lambda e: e.nop(), [d_y[0].ev(), d_y[1].ev()])
            else:
                Pf.emit("sync", lambda e: e.nop())
            Pf.check(all_cts); Pf.run_block(nc, "fin")

    return nc


def _host_tables():
    i = np.arange(128)
    tables = {}
    pk = i[:, None]
    cols = np.arange(ND * 128)[None, :]
    w = cols // 128
    pq = cols % 128
    dd = 128 * (w - 3) + pq - pk
    valid = (w >= 3) & (dd >= 0)
    sq = np.where(valid, np.sqrt(np.maximum(dd, 0).astype(np.float64)), 1e30)
    tables["sqd"] = sq.astype(np.float32)
    tables["idb"] = np.eye(128, dtype=ml_dtypes.bfloat16)
    tables["idf"] = np.eye(128, dtype=np.float32)
    bones = np.zeros((128, 2), dtype=ml_dtypes.bfloat16)
    bones[0:64, 0] = 1
    bones[64:128, 1] = 1
    tables["bonesc"] = bones
    sel = np.zeros((2, 128), dtype=ml_dtypes.bfloat16)
    sel[0, 0:64] = 1
    sel[1, 64:128] = 1
    tables["selc"] = sel
    return tables


def kernel(L0_attn_x, L0_attn_Wq, L0_attn_Wk, L0_attn_Wv, L0_attn_Wo,
           L0_ffn_w_gate_up, L0_ffn_w_proj):
    x = np.asarray(L0_attn_x, dtype=np.float32)
    wqT = np.ascontiguousarray(np.asarray(L0_attn_Wq, dtype=np.float32).T)
    wkT = np.ascontiguousarray(np.asarray(L0_attn_Wk, dtype=np.float32).T)
    wvT = np.ascontiguousarray(np.asarray(L0_attn_Wv, dtype=np.float32).T)
    woT = np.ascontiguousarray(np.asarray(L0_attn_Wo, dtype=np.float32).T)
    wguT = np.ascontiguousarray(np.asarray(L0_ffn_w_gate_up, dtype=np.float32).T)
    wpT = np.ascontiguousarray(np.asarray(L0_ffn_w_proj, dtype=np.float32).T)
    tables = _host_tables()
    nc = build_program()
    in_maps = []
    for b in range(8):
        m = {"x": np.ascontiguousarray(x[b]),
             "wqT": wqT, "wkT": wkT, "wvT": wvT, "woT": woT,
             "wguT": wguT, "wpT": wpT}
        m.update(tables)
        in_maps.append(m)
    res = run_bass_kernel_spmd(nc, in_maps, core_ids=list(range(8)))
    out = np.stack([res.results[b]["y"] for b in range(8)], axis=0)
    return out
